# version 02 (final)
# baseline (speedup 1.0000x reference)
import math
from contextlib import ExitStack

import numpy as np

import concourse.bass as bass
import concourse.mybir as mybir
from concourse.bass_utils import run_bass_kernel_spmd

F32 = mybir.dt.float32
BF16 = mybir.dt.bfloat16
ALU = mybir.AluOpType
AF = mybir.ActivationFunctionType
AX = mybir.AxisListType

B_CORE = 2
T = 2048
D = 1024
NT = T // 128

ENGINES = ("sync", "scalar", "vector", "tensor", "gpsimd")


def _flat(deps):
    out = []

    def rec(d):
        if d is None:
            return
        if isinstance(d, tuple):
            if len(d) == 2 and isinstance(d[0], str):
                out.append(d)
            else:
                for e in d:
                    rec(e)
            return
        for e in d:
            rec(e)
    rec(deps)
    return out


class Buf:

    __slots__ = ("name", "writers", "readers")

    def __init__(self, name):
        self.name = name
        self.writers = []
        self.readers = []


class Prog:
    def __init__(self, nc, ctx):
        self.nc = nc
        self.ctx = ctx
        self.streams = {e: [] for e in ENGINES}
        self.sems = {}
        self.semctr = {}
        self.waited = {e: {} for e in ENGINES}
        self.engine_sem = {"scalar": "e_scalar", "vector": "e_vector",
                           "tensor": "e_tensor"}
        for s in set(self.engine_sem.values()):
            self._mk_sem(s)
        self.queue_sems = {}
        self.queue_rr = {}
        for q in ("sync", "gpsimd"):
            self.queue_sems[q] = [f"q_{q}{i}" for i in range(8)]
            self.queue_rr[q] = 0
            for s in self.queue_sems[q]:
                self._mk_sem(s)
        self.vtrace = {e: [] for e in ENGINES}
        self.nops = 0
        self.last_writes = {e: frozenset() for e in ENGINES}

    def _mk_sem(self, name):
        self.sems[name] = self.ctx.enter_context(self.nc.semaphore(name))
        self.semctr[name] = 0

    def _sched(self, eng, make, deps, label, inc_amt, is_dma=False):
        if is_dma:
            semname = self.queue_sems[eng][self.queue_rr[eng]]
            self.queue_rr[eng] = (self.queue_rr[eng] + 1) % 8
            own = None
            if self.semctr[semname] > 0:
                deps = list(deps) + [(semname, self.semctr[semname])]
        else:
            semname = self.engine_sem[eng]
            own = semname if eng == "tensor" else None
        needed = []
        for (s, v) in _flat(deps):
            if s == own:
                continue
            if self.waited[eng].get(s, 0) < v:
                self.waited[eng][s] = v
                needed.append((s, v))
                self.vtrace[eng].append(("wait", s, v, label))
        ev = None
        if inc_amt:
            self.semctr[semname] += inc_amt
            ev = (semname, self.semctr[semname])
            self.vtrace[eng].append(("inc", semname, inc_amt, label))
        sems = self.sems
        self.nops += 1

        if is_dma:
            def run(engobj, needed=needed, make=make, ev=ev):
                for (s, v) in needed:
                    engobj.wait_ge(sems[s], v)
                make(engobj).then_inc(sems[ev[0]], 16)
        else:
            def run(engobj, needed=needed, make=make, ev=ev):
                for (s, v) in needed:
                    engobj.wait_ge(sems[s], v)
                inst = make(engobj)
                if ev is not None:
                    inst.then_inc(sems[ev[0]], 1)
        self.streams[eng].append(run)
        return ev

    def op(self, eng, make, reads=(), writes=(), deps=(), label="", inc=True):
        alldeps = list(_flat(deps))
        for b in reads:
            alldeps += b.writers
        for b in writes:
            alldeps += b.writers + b.readers
        ev = self._sched(eng, make, alldeps, label, 1 if inc else 0)
        self.last_writes[eng] = frozenset(id(b) for b in writes)
        if ev is not None:
            for b in reads:
                b.readers.append(ev)
            for b in writes:
                b.writers = [ev]
                b.readers = []
        else:
            pass
        return ev

    def group(self, eng, makes, reads=(), writes=(), deps=(), label=""):
        alldeps = list(_flat(deps))
        for b in reads:
            alldeps += b.writers
        for b in writes:
            alldeps += b.writers + b.readers
        n = len(makes)
        ev = None
        for i, mk in enumerate(makes):
            last = (i == n - 1)
            ev = self._sched(eng, mk, alldeps if i == 0 else [], f"{label}.{i}",
                             1 if last else 0)
        self.last_writes[eng] = frozenset(id(b) for b in writes)
        for b in reads:
            b.readers.append(ev)
        for b in writes:
            b.writers = [ev]
            b.readers = []
        return ev

    def dma(self, queue, out_ap, in_ap, reads=(), writes=(), deps=(), label="",
            slow_ok=False):
        alldeps = list(_flat(deps))
        for b in reads:
            alldeps += b.writers
        for b in writes:
            alldeps += b.writers + b.readers
        ev = self._sched(queue,
                         lambda e, o=out_ap, i=in_ap, s=slow_ok: e.dma_start(
                             out=o, in_=i, allow_slow_non_contiguous=s),
                         alldeps, label, 16, is_dma=True)
        for b in reads:
            b.readers.append(ev)
        for b in writes:
            b.writers = [ev]
            b.readers = []
        return ev

    def wait(self, eng, deps, label="barrier"):
        needed = []
        semname = self.engine_sem.get(eng)
        for (s, v) in _flat(deps):
            if s == semname:
                continue
            if self.waited[eng].get(s, 0) < v:
                self.waited[eng][s] = v
                needed.append((s, v))
                self.vtrace[eng].append(("wait", s, v, label))
        if not needed:
            return
        sems = self.sems

        def run(engobj, needed=needed):
            for (s, v) in needed:
                engobj.wait_ge(sems[s], v)
        self.streams[eng].append(run)

    def phase_barrier(self, label="phase"):
        evs = [(s, v) for s, v in self.semctr.items() if v > 0]
        for e in ENGINES:
            self.wait(e, evs, label=label)
            self.last_writes[e] = frozenset()

    def check_deadlock(self):
        pos = {e: 0 for e in ENGINES}
        cur = {s: 0 for s in self.semctr}
        progress = True
        while progress:
            progress = False
            for e in ENGINES:
                tr = self.vtrace[e]
                while pos[e] < len(tr):
                    item = tr[pos[e]]
                    if item[0] == "wait":
                        if cur[item[1]] < item[2]:
                            break
                    else:
                        cur[item[1]] += item[2]
                    pos[e] += 1
                    progress = True
        stuck = [(e, self.vtrace[e][pos[e]]) for e in ENGINES
                 if pos[e] < len(self.vtrace[e])]
        if stuck:
            msgs = [f"{e}: needs {it[1]}>={it[2]} (has {cur[it[1]]}) at '{it[3]}'"
                    for e, it in stuck]
            raise RuntimeError("DEADLOCK in schedule:\n" + "\n".join(msgs[:10]))

    def finalize(self):
        self.check_deadlock()
        with self.nc.Block() as block:
            @block.sync
            def _(eng):
                for fn in self.streams["sync"]:
                    fn(eng)

            @block.scalar
            def _(eng):
                for fn in self.streams["scalar"]:
                    fn(eng)

            @block.vector
            def _(eng):
                for fn in self.streams["vector"]:
                    fn(eng)

            @block.tensor
            def _(eng):
                for fn in self.streams["tensor"]:
                    fn(eng)

            @block.gpsimd
            def _(eng):
                for fn in self.streams["gpsimd"]:
                    fn(eng)


class MK:
    def __init__(self, phases=("p1",), dbg_scratch=False):
        self.dbg_scratch = dbg_scratch
        self.phases = phases
        self.nc = bass.Bass("TRN2", target_bir_lowering=False, debug=False)
        self.ctx = ExitStack()
        self.p = Prog(self.nc, self.ctx)
        self.sb = {}
        self.sbuf_bytes = 0
        self.ps = []
        self.psb = []
        self.dram = {}
        self.dbuf = {}
        self.bufs = {}
        self._declare_io()
        self._alloc_common()
        self.build()

    def _declare_io(self):
        nc, d = self.nc, self.dram

        def di(name, shape):
            d[name] = nc.dram_tensor(name, list(shape), F32, kind="ExternalInput").ap()
            self.dbuf[name] = Buf("d:" + name)
        di("x", (B_CORE, T, D))
        di("in_w", (2048, 1024)); di("in_b", (2048,))
        di("conv_w", (2048, 1, 4)); di("conv_b", (2048,))
        di("xp_w", (32, 2048)); di("xp_b", (32,))
        di("out_w", (1024, 16)); di("out_b", (1024,))
        di("Wq", (1024, 1024)); di("Wk", (1024, 1024))
        di("Wv", (1024, 2048)); di("Wg", (1024, 2048))
        di("Wgk1", (1024, 16)); di("Wgk2", (16, 1024)); di("bgk2", (1024,))
        di("rms_w", (512,)); di("Wo", (2048, 1024))
        di("wqkv", (2, 1024, 1536)); di("q_scale", (2, 64)); di("k_scale", (2, 64))
        di("wo", (2, 512, 1024)); di("ln_g", (2, 1024)); di("ln_b", (2, 1024))
        di("w1", (2, 1024, 5460)); di("b1", (2, 5460))
        di("w2", (2, 2730, 1024)); di("b2", (2, 1024))
        di("identity", (128, 128))
        di("ones_row", (1, 128))
        di("mask_lo", (128, 128))
        di("mask_hi", (128, 128))
        di("rot_cos", (T, 64)); di("rot_sin", (T, 64))
        di("rot_scale", (T, 64)); di("rot_iscale", (T, 64))
        di("gla_reset", (1, T))
        d["out"] = nc.dram_tensor("out", [B_CORE, T, D], F32, kind="ExternalOutput").ap()
        self.dbuf["out"] = Buf("d:out")
        d["dbg_xp"] = nc.dram_tensor("dbg_xp", [128, 16, 33], F32, kind="ExternalOutput").ap()
        self.dbuf["dbg_xp"] = Buf("d:dbg_xp")
        d["dbg_ys"] = nc.dram_tensor("dbg_ys", [16, T], F32, kind="ExternalOutput").ap()
        self.dbuf["dbg_ys"] = Buf("d:dbg_ys")
        d["dbg_h"] = nc.dram_tensor("dbg_h", [128, 16, 131], F32, kind="ExternalOutput").ap()
        self.dbuf["dbg_h"] = Buf("d:dbg_h")
        d["dbg_ln"] = nc.dram_tensor("dbg_ln", [128, D], F32, kind="ExternalOutput").ap()
        d["dbg_ps"] = nc.dram_tensor("dbg_ps", [128, 512], F32, kind="ExternalOutput").ap()
        self.dbuf["dbg_ps"] = Buf("d:dbg_ps")
        self.dbuf["dbg_ln"] = Buf("d:dbg_ln")
        d["dbg_lnT"] = nc.dram_tensor("dbg_lnT", [128, 8 * 128], F32, kind="ExternalOutput").ap()
        self.dbuf["dbg_lnT"] = Buf("d:dbg_lnT")

        def ds(name, shape):
            kind = "ExternalOutput" if self.dbg_scratch else "Internal"
            d[name] = nc.dram_tensor(name, list(shape), F32, kind=kind).ap()
            self.dbuf[name] = Buf("d:" + name)
        ds("XR", (B_CORE, T, D))
        ds("XR2", (B_CORE, T, D))
        ds("Y2", (B_CORE, T, D))
        ds("Y2T", (B_CORE, D, T))
        ds("QKV", (B_CORE, T, 1536))
        ds("XFa", (B_CORE, T, D))
        ds("XFb", (B_CORE, T, D))
        ds("XFT", (B_CORE, D, T))
        ds("X3T", (B_CORE, D, T))
        ds("GK", (B_CORE, D, T))
        ds("GT", (B_CORE, T, 2048))
        ds("GO", (B_CORE, 4, T, 512))

    ARENA_ELS = 14848

    def sba_persist(self, name, shape, dtype=F32):
        if name not in self.sb:
            self.sb[name] = self.nc.alloc_sbuf_tensor("sb_" + name, list(shape), dtype)
            self.bufs[name] = Buf(name)
        return self.sb[name], self.bufs[name]

    def sba(self, name, shape, dtype=F32):
        if name in self.views:
            return self.views[name]
        assert dtype == F32
        P = shape[0]
        els = 1
        for s in shape[1:]:
            els *= s
        els_al = (els + 127) // 128 * 128
        for an in ("A", "B", "C"):
            if self.acur[an] + els_al <= self.ARENA_ELS:
                off = self.acur[an]
                self.acur[an] = off + els_al
                ap = self.arenas[an][0:P, off:off + els]
                if len(shape) == 3:
                    ap = ap.rearrange("p (a b) -> p a b", a=shape[1])
                elif len(shape) == 4:
                    ap = ap.rearrange("p (a b c) -> p a b c", a=shape[1], b=shape[2])
                elif len(shape) > 4:
                    raise ValueError(shape)
                self.views[name] = (ap, Buf(name))
                return self.views[name]
        raise RuntimeError(f"arena overflow allocating {name} ({els} els); "
                           f"cursors={self.acur}")

    def new_phase(self, label):
        self.p.phase_barrier(label)
        self.views = {}
        self.acur = {"A": 0, "B": 0, "C": 0}

    def _alloc_common(self):
        for i in range(8):
            self.ps.append(self.nc.alloc_psum_tensor(f"psb{i}", [128, 512], F32))
            self.psb.append(Buf(f"ps{i}"))
        self.arenas = {an: self.nc.alloc_sbuf_tensor("arena" + an, [128, self.ARENA_ELS], F32)
                       for an in ("A", "B", "C")}
        self.views = {}
        self.acur = {"A": 0, "B": 0, "C": 0}
        self.sba_persist("identity", (128, 128))
        self.sba_persist("ones_row", (1, 128))

        self.sba_persist("sq_scr", (128, D))
        self.sba_persist("stat", (128, 40))
        self.sba_persist("xin0", (128, D))
        self.sba_persist("xin1", (128, D))
        self.sba_persist("ln", (128, D))
        self.sba_persist("lnT", (128, 8 * 128))
        self.sba_persist("res", (128, D))
        p = self.p
        p.dma("sync", self.sb["identity"][:], self.dram["identity"],
              writes=[self.bufs["identity"]], label="ld_idn")
        p.dma("sync", self.sb["ones_row"][:], self.dram["ones_row"],
              writes=[self.bufs["ones_row"]], label="ld_ones")

    def mm(self, out_ap, lhsT_ap, rhs_ap, start, stop):
        return lambda e: e.matmul(out_ap, lhsT_ap, rhs_ap, start=start, stop=stop)

    def transpose_many(self, items, bank0, bank1, label, evac_eng="vector"):
        p = self.p
        idn = self.sb["identity"]
        idnb = self.bufs["identity"]
        for i, it in enumerate(items):
            in_ap, K, M, dst, in_bufs = it
            bank = (bank0, bank1)[i % 2]
            ps, pb = self.ps[bank], self.psb[bank]
            p.op("tensor",
                 lambda e, in_ap=in_ap, K=K, M=M, ps=ps: e.transpose(
                     out=ps[0:M, 0:K], in_=in_ap, identity=idn[0:K, 0:K]),
                 reads=[idnb] + list(in_bufs), writes=[pb], label=f"{label}:t{i}")
            if callable(dst):
                dst(pb, ps[0:M, 0:K], i)
            else:
                dst_ap, dst_buf = dst
                p.op(evac_eng,
                     lambda e, dst_ap=dst_ap, ps=ps, K=K, M=M: e.tensor_copy(
                         out=dst_ap, in_=ps[0:M, 0:K]),
                     reads=[pb], writes=[dst_buf], label=f"{label}:e{i}")

    def ln_stats(self, xin_ap, xin_buf, stat, statb, col0, tag):
        p = self.p
        sq, sqb = self.sba_persist("sq_scr", (128, D))
        c = col0
        p.op("vector", lambda e: e.reduce_sum(out=stat[:, c:c + 1], in_=xin_ap, axis=AX.X),
             reads=[xin_buf], writes=[statb], label=f"{tag}:ssum")
        p.op("scalar", lambda e: e.activation(out=sq[:], in_=xin_ap, func=AF.Square,
                                              accum_out=stat[:, c + 1:c + 2]),
             reads=[xin_buf], writes=[statb, sqb], label=f"{tag}:ssq")
        p.op("vector", lambda e: e.tensor_scalar(out=stat[:, c + 2:c + 3], in0=stat[:, c:c + 1],
                                                 scalar1=1.0 / D, scalar2=None, op0=ALU.mult),
             reads=[statb], writes=[statb], label=f"{tag}:mu")
        p.op("vector", lambda e: e.tensor_scalar(out=stat[:, c + 3:c + 4], in0=stat[:, c + 1:c + 2],
                                                 scalar1=1.0 / D, scalar2=1e-5,
                                                 op0=ALU.mult, op1=ALU.add),
             reads=[statb], writes=[statb], label=f"{tag}:v1")
        p.op("vector", lambda e: e.tensor_tensor(out=stat[:, c + 4:c + 5], in0=stat[:, c + 2:c + 3],
                                                 in1=stat[:, c + 2:c + 3], op=ALU.mult),
             reads=[statb], writes=[statb], label=f"{tag}:mu2")
        p.op("vector", lambda e: e.tensor_sub(out=stat[:, c + 5:c + 6], in0=stat[:, c + 3:c + 4],
                                              in1=stat[:, c + 4:c + 5]),
             reads=[statb], writes=[statb], label=f"{tag}:var")
        p.op("scalar", lambda e: e.activation(out=stat[:, c + 6:c + 7], in_=stat[:, c + 5:c + 6],
                                              func=AF.Sqrt),
             reads=[statb], writes=[statb], label=f"{tag}:std")
        p.op("vector", lambda e: e.reciprocal(out=stat[:, c + 7:c + 8], in_=stat[:, c + 6:c + 7]),
             reads=[statb], writes=[statb], label=f"{tag}:inv")
        p.op("vector", lambda e: e.scalar_tensor_tensor(out=stat[:, c + 8:c + 9],
                                                        in0=stat[:, c + 2:c + 3], scalar=-1.0,
                                                        in1=stat[:, c + 7:c + 8],
                                                        op0=ALU.mult, op1=ALU.mult),
             reads=[statb], writes=[statb], label=f"{tag}:nb")
        return stat[:, c + 7:c + 8], stat[:, c + 8:c + 9]

    def build(self):
        steps = []
        cur = "x"
        names = list(self.phases)
        for i, ph in enumerate(names):
            last = (i == len(names) - 1)
            if ph == "p1":
                dst = "out" if last else "XR"
            elif ph == "gla":
                dst = "out" if last else "XR"
            else:
                dst = "out" if last else ("XR2" if cur != "XR2" else "XR")
            steps.append((ph, cur, dst, last))
            cur = dst
        if self.phases == ("qkvmin",):
            self.tp_qkvmin()
            self.final_store()
            self.p.finalize()
            self.ctx.close()
            return
        for ph, src, dst, last in steps:
            if ph == "p1":
                self.phase1(src, dst, last)
            elif ph.startswith("ffn"):
                self.ffn_block(src, dst, last, ph)
            elif ph == "gla":
                self.gla_block(src, dst, last, ph)
        self.final_store()
        self.p.finalize()
        self.ctx.close()

    def final_store(self):
        self.p.wait("sync", self.dbuf["out"].writers + self.dbuf["out"].readers
                    + self.dbuf["dbg_xp"].writers + self.dbuf["dbg_ys"].writers
                    + self.dbuf["dbg_h"].writers + self.dbuf["dbg_ln"].writers
                    + self.dbuf["dbg_lnT"].writers,
                    label="final_out")

    def phase1(self, src="x", dst="XR", final=False):
        nc, p, d = self.nc, self.p, self.dram
        sba = self.sba
        idn, idnb = self.sb["identity"], self.bufs["identity"]
        ones, onesb = self.sb["ones_row"], self.bufs["ones_row"]

        self.new_phase("p1")
        inwT_a, inwT_ab = sba("p1_inwT_a", (128, 4 * 2048))
        inwT_b, inwT_bb = sba("p1_inwT_b", (128, 4 * 2048))

        def inwT_ap(k, c):
            t = inwT_a if k < 4 else inwT_b
            kk = k % 4
            return t[:, kk * 2048 + c * 128: kk * 2048 + (c + 1) * 128]
        inwT_bufs = [inwT_ab, inwT_bb]
        inb, inbb = sba("p1_inb", (128, 16))
        convw, convwb = sba("p1_convw", (128, 16 * 4))
        convb, convbb = sba("p1_convb", (128, 16))
        xpwT, xpwTb = sba("p1_xpwT", (128, 16 * 33))
        xpbr, xpbrb = sba("p1_xpb_row", (1, 33))
        outwT, outwTb = sba("p1_outwT", (16, 1024))
        outbr, outbrb = sba("p1_outb_row", (1, 1024))
        h_cm, h_cmb = sba("p1_h_cm", (128, 16, 131))
        hc_cm, hc_cmb = sba("p1_hc_cm", (128, 16, 128))
        xp_all, xp_allb = sba("p1_xp_all", (128, 16, 33))
        e1t, e1b = sba("p1_e1", (128, 16, 16))
        a_t, a_tb = sba("p1_a_t", (128, 16, 16))
        b_t, b_tb = sba("p1_b_t", (128, 16, 16))
        a_cm, a_cmb = sba("p1_a_cm", (16, T))
        b_cm, b_cmb = sba("p1_b_cm", (16, T))
        ys_cm, ys_cmb = sba("p1_ys_cm", (16, T))
        wstage, wstageb = sba("p1_wstage", (128, 2048))
        xin0, xin0b = self.sba_persist("xin0", (128, D))
        xin1, xin1b = self.sba_persist("xin1", (128, D))
        ln, lnb = self.sba_persist("ln", (128, D))
        lnT, lnTb = self.sba_persist("lnT", (128, 8 * 128))
        res, resb = self.sba_persist("res", (128, D))
        stat, statb = self.sba_persist("stat", (128, 40))

        wstage2, wstage2b = sba("p1_wstage2", (128, 1024))
        for c in range(16):
            stg, stgb = ((wstage, wstageb) if c % 2 == 0 else (wstage2, wstage2b))
            off = 0
            p.dma("sync", stg[:, off:off + 1024], d["in_w"][c * 128:(c + 1) * 128, :],
                  writes=[stgb], label=f"p1:ld_inw{c}")
            items = [(stg[:, k * 128:(k + 1) * 128], 128, 128,
                      (inwT_ap(k, c), inwT_bufs[k // 4]),
                      [stgb]) for k in range(8)]
            self.transpose_many(items, 0, 1, f"p1:inwT{c}")

        p.dma("sync", inb[:], d["in_b"].rearrange("(c p) -> p c", p=128),
              writes=[inbb], label="p1:ld_inb", slow_ok=True)
        p.dma("sync", convw[:].rearrange("p (c j) -> p c j", c=16),
              d["conv_w"].rearrange("(c p) o j -> p c (o j)", p=128),
              writes=[convwb], label="p1:ld_convw", slow_ok=True)
        p.dma("sync", convb[:], d["conv_b"].rearrange("(c p) -> p c", p=128),
              writes=[convbb], label="p1:ld_convb", slow_ok=True)
        p.dma("sync", wstage[0:32, 0:2048], d["xp_w"], writes=[wstageb], label="p1:ld_xpw")
        items = []
        for c in range(16):
            items.append((wstage[0:32, c * 128:(c + 1) * 128], 32, 128,
                          (xpwT[:, c * 33: c * 33 + 32], xpwTb), [wstageb]))
        self.transpose_many(items, 2, 3, "p1:xpwT")
        p.op("vector", lambda e: e.memset(
            xpwT[:].rearrange("p (c u) -> p c u", c=16)[:, :, 32:33], 1.0),
            writes=[xpwTb], label="p1:xpwT_ones")
        p.dma("sync", xpbr[0:1, 0:32], d["xp_b"].rearrange("(o c) -> o c", o=1),
              writes=[xpbrb], label="p1:ld_xpb")
        p.op("vector", lambda e: e.memset(xpbr[0:1, 32:33], 0.0),
             writes=[xpbrb], label="p1:xpb_z")
        p.dma("sync", wstage[:, 0:8 * 16].rearrange("p (c s) -> p c s", c=8),
              d["out_w"].rearrange("(c p) s -> p c s", p=128),
              writes=[wstageb], label="p1:ld_outw")
        items = []
        for c in range(8):
            items.append((wstage[:, c * 16:(c + 1) * 16], 128, 16,
                          (outwT[0:16, c * 128:(c + 1) * 128], outwTb), [wstageb]))
        self.transpose_many(items, 4, 5, "p1:owT")
        p.dma("sync", outbr[:], d["out_b"].rearrange("(o c) -> o c", o=1),
              writes=[outbrb], label="p1:ld_outb")

        x_d, xb = d[src], self.dbuf[src]
        xr_d, xrb = d[dst], self.dbuf[dst]

        for b in range(B_CORE):
            p.op("vector", lambda e: e.memset(h_cm[:, :, 0:3], 0.0),
                 writes=[h_cmb], label=f"p1:b{b}:hcm0")
            for t in range(NT):
                xin, xinb = (xin0, xin0b) if t % 2 == 0 else (xin1, xin1b)
                p.dma("sync", xin[:], x_d[b, t * 128:(t + 1) * 128, :],
                      reads=[xb], writes=[xinb], label=f"p1:b{b}t{t}:ldx")
                inv_ap, nb_ap = self.ln_stats(xin[:], xinb, stat, statb,
                                              (t % 2) * 12, f"p1:b{b}t{t}")
                p.op("scalar", lambda e, xin=xin, inv_ap=inv_ap, nb_ap=nb_ap:
                     e.activation(out=ln[:], in_=xin[:], func=AF.Identity,
                                  scale=inv_ap, bias=nb_ap),
                     reads=[xinb, statb], writes=[lnb], label=f"p1:b{b}t{t}:ln")
                items = [(ln[:, i * 128:(i + 1) * 128], 128, 128,
                          (lnT[:, i * 128:(i + 1) * 128], lnTb), [lnb])
                         for i in range(8)]
                self.transpose_many(items, 0, 1, f"p1:b{b}t{t}:lnT")
                if t > 0:
                    p.op("vector", lambda e: e.tensor_copy(out=h_cm[:, :, 0:3],
                                                           in_=h_cm[:, :, 128:131]),
                         reads=[h_cmb], writes=[h_cmb], label=f"p1:b{b}t{t}:tail")
                for c in range(16):
                    bank = 4 + (c % 2)
                    psq, pqb = self.ps[bank], self.psb[bank]
                    makes = [self.mm(psq[0:128, 0:128],
                                     inwT_ap(k, c),
                                     lnT[:, k * 128:(k + 1) * 128],
                                     start=(k == 0), stop=(k == 7))
                             for k in range(8)]
                    p.group("tensor", makes, reads=inwT_bufs + [lnTb], writes=[pqb],
                            label=f"p1:b{b}t{t}:h{c}")
                    p.op("scalar",
                         lambda e, c=c, psq=psq: e.activation(
                             out=h_cm[:, c, 3:131], in_=psq[0:128, 0:128],
                             func=AF.Identity, bias=inb[:, c:c + 1]),
                         reads=[pqb, inbb], writes=[h_cmb], label=f"p1:b{b}t{t}:hev{c}")
                for c in range(16):
                    p.op("vector",
                         lambda e, c=c: e.tensor_scalar(
                             out=hc_cm[:, c, :], in0=h_cm[:, c, 0:128],
                             scalar1=convw[:, c * 4: c * 4 + 1],
                             scalar2=convb[:, c:c + 1], op0=ALU.mult, op1=ALU.add),
                         reads=[h_cmb, convwb, convbb], writes=[hc_cmb],
                         label=f"p1:b{b}t{t}:cv{c}a")
                    for j in (1, 2, 3):
                        p.op("vector",
                             lambda e, c=c, j=j: e.scalar_tensor_tensor(
                                 out=hc_cm[:, c, :], in0=h_cm[:, c, j:j + 128],
                                 scalar=convw[:, c * 4 + j: c * 4 + j + 1],
                                 in1=hc_cm[:, c, :], op0=ALU.mult, op1=ALU.add),
                             reads=[h_cmb, convwb, hc_cmb], writes=[hc_cmb],
                             label=f"p1:b{b}t{t}:cv{c}.{j}")
                p.op("scalar", lambda e: e.activation(out=hc_cm[:], in_=hc_cm[:],
                                                      func=AF.Silu),
                     reads=[hc_cmb], writes=[hc_cmb], label=f"p1:b{b}t{t}:silu")
                psx, pxb = self.ps[6], self.psb[6]
                makes = [self.mm(psx[0:128, 0:33], hc_cm[:, k, :],
                                 xpwT[:, k * 33:(k + 1) * 33],
                                 start=(k == 0), stop=False) for k in range(16)]
                makes.append(self.mm(psx[0:128, 0:33], ones[:], xpbr[:],
                                     start=False, stop=True))
                p.group("tensor", makes, reads=[hc_cmb, xpwTb, onesb, xpbrb],
                        writes=[pxb], label=f"p1:b{b}t{t}:xp")
                p.op("vector", lambda e, t=t: e.tensor_copy(out=xp_all[:, t, :],
                                                            in_=psx[0:128, 0:33]),
                     reads=[pxb], writes=[xp_allb], label=f"p1:b{b}t{t}:xpev")

            p.op("scalar", lambda e: e.activation(out=e1t[:], in_=xp_all[:, :, 0:16],
                                                  func=AF.Exp),
                 reads=[xp_allb], writes=[e1b], label=f"p1:b{b}:e1")
            p.op("scalar", lambda e: e.activation(out=a_t[:], in_=e1t[:], func=AF.Exp,
                                                  scale=-1.0),
                 reads=[e1b], writes=[a_tb], label=f"p1:b{b}:a")
            p.op("scalar", lambda e: e.activation(out=b_t[:], in_=xp_all[:, :, 16:32],
                                                  func=AF.Silu),
                 reads=[xp_allb], writes=[b_tb], label=f"p1:b{b}:sl")
            p.op("vector", lambda e: e.tensor_tensor(
                out=b_t[:], in0=b_t[:],
                in1=xp_all[:, :, 32:33].broadcast_to((128, 16, 16)), op=ALU.mult),
                reads=[b_tb, xp_allb], writes=[b_tb], label=f"p1:b{b}:bt")
            if b == 0:
                p.dma("gpsimd", d["dbg_xp"], xp_all[:], reads=[xp_allb],
                      writes=[self.dbuf["dbg_xp"]], label="dbg_xp")
                p.dma("gpsimd", d["dbg_h"], h_cm[:], reads=[h_cmb],
                      writes=[self.dbuf["dbg_h"]], label="dbg_h")
                p.dma("gpsimd", d["dbg_ln"], ln[:], reads=[lnb],
                      writes=[self.dbuf["dbg_ln"]], label="dbg_ln")
                p.dma("gpsimd", d["dbg_lnT"], lnT[:], reads=[lnTb],
                      writes=[self.dbuf["dbg_lnT"]], label="dbg_lnT")
            items = []
            for t in range(NT):
                items.append((a_t[:, t, :], 128, 16,
                              (a_cm[0:16, t * 128:(t + 1) * 128], a_cmb), [a_tb]))
                items.append((b_t[:, t, :], 128, 16,
                              (b_cm[0:16, t * 128:(t + 1) * 128], b_cmb), [b_tb]))
            self.transpose_many(items, 6, 7, f"p1:b{b}:abT")
            p.op("vector", lambda e: e.tensor_tensor_scan(
                out=ys_cm[0:16, :], data0=a_cm[0:16, :], data1=b_cm[0:16, :],
                initial=0.0, op0=ALU.mult, op1=ALU.add),
                reads=[a_cmb, b_cmb], writes=[ys_cmb], label=f"p1:b{b}:scan")
            if b == 0:
                p.dma("gpsimd", d["dbg_ys"], ys_cm[0:16, :], reads=[ys_cmb],
                      writes=[self.dbuf["dbg_ys"]], label="dbg_ys")

            for t in range(NT):
                xin, xinb = (xin0, xin0b) if t % 2 == 0 else (xin1, xin1b)
                p.dma("sync", xin[:], x_d[b, t * 128:(t + 1) * 128, :],
                      reads=[xb], writes=[xinb], label=f"p1:b{b}t{t}:ldx2")
                for c in range(2):
                    pso, pob = self.ps[c], self.psb[c]
                    makes = [self.mm(pso[0:128, 0:512],
                                     ys_cm[0:16, t * 128:(t + 1) * 128],
                                     outwT[0:16, c * 512:(c + 1) * 512],
                                     start=True, stop=False),
                             self.mm(pso[0:128, 0:512], ones[:],
                                     outbr[0:1, c * 512:(c + 1) * 512],
                                     start=False, stop=True)]
                    p.group("tensor", makes, reads=[ys_cmb, outwTb, onesb, outbrb],
                            writes=[pob], label=f"p1:b{b}t{t}:op{c}")
                    p.op("vector",
                         lambda e, c=c, xin=xin, pso=pso: e.tensor_add(
                             out=res[:, c * 512:(c + 1) * 512],
                             in0=pso[0:128, 0:512], in1=xin[:, c * 512:(c + 1) * 512]),
                         reads=[pob, xinb], writes=[resb], label=f"p1:b{b}t{t}:rev{c}")
                p.dma("gpsimd", xr_d[b, t * 128:(t + 1) * 128, :], res[:],
                      reads=[resb], writes=[xrb], label=f"p1:b{b}t{t}:stXR")

    def ln_transpose_phase(self, src, dst_tm, dst_fmT, tag):
        p, d = self.p, self.dram
        self.new_phase(tag)
        xin0, xin0b = self.sba_persist("xin0", (128, D))
        xin1, xin1b = self.sba_persist("xin1", (128, D))
        ln, lnb = self.sba_persist("ln", (128, D))
        lnT, lnTb = self.sba_persist("lnT", (128, 8 * 128))
        stat, statb = self.sba_persist("stat", (128, 40))
        src_d, srcb = d[src], self.dbuf[src]
        for b in range(B_CORE):
            for t in range(NT):
                xin, xinb = (xin0, xin0b) if t % 2 == 0 else (xin1, xin1b)
                p.dma("sync", xin[:], src_d[b, t * 128:(t + 1) * 128, :],
                      reads=[srcb], writes=[xinb], label=f"{tag}:b{b}t{t}:ld")
                inv_ap, nb_ap = self.ln_stats(xin[:], xinb, stat, statb,
                                              (t % 2) * 12, f"{tag}:b{b}t{t}")
                p.op("scalar", lambda e, xin=xin, inv_ap=inv_ap, nb_ap=nb_ap:
                     e.activation(out=ln[:], in_=xin[:], func=AF.Identity,
                                  scale=inv_ap, bias=nb_ap),
                     reads=[xinb, statb], writes=[lnb], label=f"{tag}:b{b}t{t}:ln")
                if dst_tm is not None:
                    p.dma("gpsimd", d[dst_tm][b, t * 128:(t + 1) * 128, :], ln[:],
                          reads=[lnb], writes=[self.dbuf[dst_tm]],
                          label=f"{tag}:b{b}t{t}:st")
                items = [(ln[:, i * 128:(i + 1) * 128], 128, 128,
                          (lnT[:, i * 128:(i + 1) * 128], lnTb), [lnb])
                         for i in range(8)]
                self.transpose_many(items, 0, 1, f"{tag}:b{b}t{t}:T")
                for kc in range(8):
                    p.dma("gpsimd",
                          d[dst_fmT][b, kc * 128:(kc + 1) * 128, t * 128:(t + 1) * 128],
                          lnT[:, kc * 128:(kc + 1) * 128],
                          reads=[lnTb], writes=[self.dbuf[dst_fmT]],
                          label=f"{tag}:b{b}t{t}:stT{kc}")

    def ffn_qkv(self, l, inT, tag):
        p, d, sba = self.p, self.dram, self.sba
        self.new_phase(tag)
        ones, onesb = self.sb["ones_row"], self.bufs["ones_row"]
        wq, wqb = sba("wq", (128, 8, 1536))
        p.dma("sync", wq[:], d["wqkv"][l].rearrange("(k p) c -> p k c", p=128),
              writes=[wqb], label=f"{tag}:ldw")
        rcos, rcosb = sba("rcos", (128, 16, 64))
        rsin, rsinb = sba("rsin", (128, 16, 64))
        rsc, rscb = sba("rsc", (128, 16, 64))
        risc, riscb = sba("risc", (128, 16, 64))
        SKIPROT = getattr(self, "dbg_skip_rot", False)
        for nm, (tt, bb_) in (() if SKIPROT else (("rot_cos", (rcos, rcosb)), ("rot_sin", (rsin, rsinb)),
                              ("rot_scale", (rsc, rscb)), ("rot_iscale", (risc, riscb)))):
            p.dma("sync", tt[:], d[nm].rearrange("(t p) c -> p t c", p=128),
                  writes=[bb_], label=f"{tag}:ld_{nm}")
        qs, qsb = sba("qs", (128, 64))
        ks, ksb = sba("ks", (128, 64))
        qsr, qsrb = sba("qsr", (128, 64))
        ksr, ksrb = sba("ksr", (128, 64))
        if not SKIPROT:
            p.dma("sync", qs[:], d["q_scale"][l].rearrange("(o c) -> o c", o=1).broadcast_to((128, 64)),
                  writes=[qsb], label=f"{tag}:ld_qs")
            p.dma("sync", ks[:], d["k_scale"][l].rearrange("(o c) -> o c", o=1).broadcast_to((128, 64)),
                  writes=[ksb], label=f"{tag}:ld_ks")
            p.dma("sync", qsr[:, 0:32], d["q_scale"][l].rearrange("(o c) -> o c", o=1)[:, 32:64].broadcast_to((128, 32)),
                  writes=[qsrb], label=f"{tag}:ld_qsr1")
            p.dma("sync", qsr[:, 32:64], d["q_scale"][l].rearrange("(o c) -> o c", o=1)[:, 0:32].broadcast_to((128, 32)),
                  writes=[qsrb], label=f"{tag}:ld_qsr2")
            p.dma("sync", ksr[:, 0:32], d["k_scale"][l].rearrange("(o c) -> o c", o=1)[:, 32:64].broadcast_to((128, 32)),
                  writes=[ksrb], label=f"{tag}:ld_ksr1")
            p.dma("sync", ksr[:, 32:64], d["k_scale"][l].rearrange("(o c) -> o c", o=1)[:, 0:32].broadcast_to((128, 32)),
                  writes=[ksrb], label=f"{tag}:ld_ksr2")
        cqq, cqqb = sba("cqq", (128, 16, 64))
        sqq, sqqb = sba("sqq", (128, 16, 64))
        ckk, ckkb = sba("ckk", (128, 16, 64))
        skk, skkb = sba("skk", (128, 16, 64))
        tmp, tmpb = sba("ttmp", (128, 16, 64))
        for (srcA, srcAb, srcS, srcSb, scl, sclb, fac, dst, dstb, nm) in (() if SKIPROT else (
                (rcos, rcosb, rsc, rscb, qs, qsb, 8.0, cqq, cqqb, "cqq"),
                (rsin, rsinb, rsc, rscb, qsr, qsrb, 8.0, sqq, sqqb, "sqq"),
                (rcos, rcosb, risc, riscb, ks, ksb, 1.0, ckk, ckkb, "ckk"),
                (rsin, rsinb, risc, riscb, ksr, ksrb, 1.0, skk, skkb, "skk"))):
            p.op("vector", lambda e, a=srcA, s=srcS: e.tensor_tensor(
                out=tmp[:], in0=a[:], in1=s[:], op=ALU.mult),
                reads=[srcAb, srcSb], writes=[tmpb], label=f"{tag}:{nm}1")
            p.op("vector", lambda e, scl=scl, fac=fac, dst=dst: e.scalar_tensor_tensor(
                out=dst[:], in0=tmp[:], scalar=fac,
                in1=scl[:].rearrange("p (a c) -> p a c", a=1).broadcast_to((128, 16, 64)),
                op0=ALU.mult, op1=ALU.mult),
                reads=[tmpb, sclb], writes=[dstb], label=f"{tag}:{nm}2")
        ysh, yshb = sba("ysh", (128, 8, 128))
        qh, qhb = sba("qh", (128, 8, 64))
        kh, khb = sba("kh", (128, 8, 64))
        t1, t1b = sba("t1", (128, 8, 64))
        r1, r1b = sba("r1", (128, 8, 32))
        r2, r2b = sba("r2", (128, 8, 32))
        vsb, vsbb = sba("vsb", (128, 512))
        sst, sstb = sba("sst", (128, 48))
        inT_d, inTb = d[inT], self.dbuf[inT]
        qkv_d, qkvb = d["QKV"], self.dbuf["QKV"]
        d_dbg_lnT = d["dbg_lnT"]

        def rot(src, srcb_, ctab, ctabb, stab, stabb, t, dst_cols, b, nm):
            p.op("vector", lambda e: e.tensor_tensor(
                out=t1[:], in0=src[:],
                in1=ctab[:, t:t + 1, :].broadcast_to((128, 8, 64)), op=ALU.mult),
                reads=[srcb_, ctabb], writes=[t1b], label=f"{nm}:c")
            p.op("vector", lambda e: e.tensor_tensor(
                out=r1[:], in0=src[:, :, 32:64],
                in1=stab[:, t:t + 1, 0:32].broadcast_to((128, 8, 32)), op=ALU.mult),
                reads=[srcb_, stabb], writes=[r1b], label=f"{nm}:r1")
            p.op("vector", lambda e: e.tensor_tensor(
                out=r2[:], in0=src[:, :, 0:32],
                in1=stab[:, t:t + 1, 32:64].broadcast_to((128, 8, 32)), op=ALU.mult),
                reads=[srcb_, stabb], writes=[r2b], label=f"{nm}:r2")
            p.op("vector", lambda e: e.tensor_sub(
                out=t1[:, :, 0:32], in0=t1[:, :, 0:32], in1=r1[:]),
                reads=[t1b, r1b], writes=[t1b], label=f"{nm}:s1")
            p.op("vector", lambda e: e.tensor_add(
                out=t1[:, :, 32:64], in0=t1[:, :, 32:64], in1=r2[:]),
                reads=[t1b, r2b], writes=[t1b], label=f"{nm}:s2")
            p.dma("gpsimd", qkv_d[b, t * 128:(t + 1) * 128, dst_cols[0]:dst_cols[1]],
                  t1[:], reads=[t1b], writes=[qkvb], label=f"{nm}:st")

        for b in range(B_CORE):
            for t in range(NT):
                for kc in range(4):
                    p.dma("sync", ysh[:, kc, :],
                          inT_d[b, kc * 128:(kc + 1) * 128, t * 128:(t + 1) * 128],
                          reads=[inTb], writes=[yshb], label=f"{tag}:b{b}t{t}:lda{kc}")
                for kc in range(4, 8):
                    if t == 0:
                        p.dma("sync", ysh[:, kc, 1:128],
                              inT_d[b, kc * 128:(kc + 1) * 128, 0:127],
                              reads=[inTb], writes=[yshb], label=f"{tag}:b{b}t{t}:ldb{kc}")
                    else:
                        p.dma("sync", ysh[:, kc, :],
                              inT_d[b, kc * 128:(kc + 1) * 128, t * 128 - 1:(t + 1) * 128 - 1],
                              reads=[inTb], writes=[yshb], label=f"{tag}:b{b}t{t}:ldb{kc}")
                if t == 0:
                    p.op("vector", lambda e: e.memset(ysh[:, 4:8, 0:1], 0.0),
                         writes=[yshb], label=f"{tag}:b{b}t{t}:z")
                for c in range(3):
                    psq, pqb = self.ps[c], self.psb[c]
                    makes = [self.mm(psq[0:128, 0:512], ysh[:, k, :],
                                     wq[:, k, c * 512:(c + 1) * 512],
                                     start=(k == 0), stop=(k == 7)) for k in range(8)]
                    p.group("tensor", makes, reads=[yshb, wqb], writes=[pqb],
                            label=f"{tag}:b{b}t{t}:mm{c}")
                p.op("scalar", lambda e: e.activation(out=vsb[:], in_=self.ps[2][0:128, 0:512],
                                                      func=AF.Identity),
                     reads=[self.psb[2]], writes=[vsbb], label=f"{tag}:b{b}t{t}:vev")
                p.dma("gpsimd", qkv_d[b, t * 128:(t + 1) * 128, 1024:1536], vsb[:],
                      reads=[vsbb], writes=[qkvb], label=f"{tag}:b{b}t{t}:vst")
                for (ci, col) in ((0, 0), (1, 8)):
                    psq, pqb = self.ps[ci], self.psb[ci]
                    p.op("scalar", lambda e, psq=psq: e.activation(
                        out=t1[:], in_=psq[0:128, 0:512].rearrange("p (h c) -> p h c", h=8),
                        func=AF.Square),
                        reads=[pqb], writes=[t1b], label=f"{tag}:b{b}t{t}:sq{ci}")
                    p.op("vector", lambda e, col=col: e.reduce_sum(
                        out=sst[:, col:col + 8].rearrange("p (h o) -> p h o", o=1),
                        in_=t1[:], axis=AX.X),
                        reads=[t1b], writes=[sstb], label=f"{tag}:b{b}t{t}:ssq{ci}")
                    p.op("scalar", lambda e, col=col: e.activation(
                        out=sst[:, col + 16:col + 24], in_=sst[:, col:col + 8], func=AF.Sqrt),
                        reads=[sstb], writes=[sstb], label=f"{tag}:b{b}t{t}:nrm{ci}")
                    p.op("vector", lambda e, col=col: e.tensor_scalar_max(
                        out=sst[:, col + 16:col + 24], in0=sst[:, col + 16:col + 24],
                        scalar1=1e-12),
                        reads=[sstb], writes=[sstb], label=f"{tag}:b{b}t{t}:clip{ci}")
                    p.op("vector", lambda e, col=col: e.reciprocal(
                        out=sst[:, col + 32:col + 40], in_=sst[:, col + 16:col + 24]),
                        reads=[sstb], writes=[sstb], label=f"{tag}:b{b}t{t}:rec{ci}")
                for (ci, dst, dstb, col) in ((0, qh, qhb, 0), (1, kh, khb, 8)):
                    psq, pqb = self.ps[ci], self.psb[ci]
                    p.op("vector", lambda e, psq=psq, dst=dst, col=col: e.tensor_tensor(
                        out=dst[:], in0=psq[0:128, 0:512].rearrange("p (h c) -> p h c", h=8),
                        in1=sst[:, col + 32:col + 40].rearrange("p (h o) -> p h o", o=1)
                            .broadcast_to((128, 8, 64)), op=ALU.mult),
                        reads=[pqb, sstb], writes=[dstb], label=f"{tag}:b{b}t{t}:nm{ci}")
                if SKIPROT:
                    p.dma("gpsimd", qkv_d[b, t * 128:(t + 1) * 128, 0:512], qh[:].rearrange("p a c -> p (a c)"),
                          reads=[qhb], writes=[qkvb], label=f"{tag}:b{b}t{t}:sq")
                    p.dma("gpsimd", qkv_d[b, t * 128:(t + 1) * 128, 512:1024], kh[:].rearrange("p a c -> p (a c)"),
                          reads=[khb], writes=[qkvb], label=f"{tag}:b{b}t{t}:sk")
                else:
                    rot(qh, qhb, cqq, cqqb, sqq, sqqb, t, (0, 512), b, f"{tag}:b{b}t{t}:rq")
                    rot(kh, khb, ckk, ckkb, skk, skkb, t, (512, 1024), b, f"{tag}:b{b}t{t}:rk")

    def ffn_attn(self, l, resid, dst, tag):
        p, d, sba = self.p, self.dram, self.sba
        self.new_phase(tag)
        idn, idnb = self.sb["identity"], self.bufs["identity"]
        res, resb = self.sba_persist("res", (128, D))
        wo_sb, wob = sba("wo_sb", (128, 4, 1024))
        p.dma("sync", wo_sb[:], d["wo"][l].rearrange("(k p) c -> p k c", p=128),
              writes=[wob], label=f"{tag}:ldwo")
        mlo, mlob = sba("mlo", (128, 128))
        mhi, mhib = sba("mhi", (128, 128))
        p.dma("sync", mlo[:], d["mask_lo"], writes=[mlob], label=f"{tag}:ldml")
        p.dma("sync", mhi[:], d["mask_hi"], writes=[mhib], label=f"{tag}:ldmh")
        kroll, krollb = sba("kroll", (64, 5, 8, 128))
        vroll, vrollb = sba("vroll", (128, 5, 8, 65))
        qkvt, qkvtb = sba("qkvt", (128, 1536))
        qT, qTb = sba("qT", (64, 8, 128))
        PT, PTb = sba("PT", (128, 5, 128))
        osb, osbb = sba("osb", (128, 512))
        oT, oTb = sba("oT", (128, 4, 128))
        rec, recb = sba("rec", (128, 8))
        ytile, ytileb = sba("ytile", (128, D))
        qkv_d, qkvb = d["QKV"], self.dbuf["QKV"]
        res_d, residb = d[resid], self.dbuf[resid]
        dst_d, dstb_ = d[dst], self.dbuf[dst]
        for b in range(B_CORE):
            p.op("vector", lambda e: e.memset(vroll[:, :, :, 64:65], 1.0),
                 writes=[vrollb], label=f"{tag}:b{b}:vones")
            for t in range(NT):
                slot = t % 5
                p.dma("sync", qkvt[:], qkv_d[b, t * 128:(t + 1) * 128, :],
                      reads=[qkvb], writes=[qkvtb], label=f"{tag}:b{b}t{t}:ld")
                for h in range(8):
                    p.op("vector", lambda e, h=h, slot=slot: e.tensor_copy(
                        out=vroll[:, slot, h, 0:64],
                        in_=qkvt[:, 1024 + h * 64:1024 + (h + 1) * 64]),
                        reads=[qkvtb], writes=[vrollb], label=f"{tag}:b{b}t{t}:vc{h}")
                items = []
                for h in range(8):
                    items.append((qkvt[:, 512 + h * 64:512 + (h + 1) * 64], 128, 64,
                                  (kroll[0:64, slot, h, :], krollb), [qkvtb]))
                    items.append((qkvt[:, h * 64:(h + 1) * 64], 128, 64,
                                  (qT[0:64, h, :], qTb), [qkvtb]))
                self.transpose_many(items, 0, 1, f"{tag}:b{b}t{t}:T")
                mfirst = max(0, 4 - t)
                for h in range(8):
                    for m in range(mfirst, 5):
                        js = (t - 4 + m) % 5
                        bank, qoff = (3, (m % 4) * 128) if m < 4 else (4, 0)
                        psq, pqb = self.ps[bank], self.psb[bank]
                        p.op("tensor", self.mm(psq[0:128, qoff:qoff + 128],
                                               kroll[0:64, js, h, :], qT[0:64, h, :],
                                               start=True, stop=True),
                             reads=[krollb, qTb], writes=[pqb],
                             label=f"{tag}:b{b}t{t}h{h}:st{m}")
                    for m in range(mfirst, 5):
                        bank, qoff = (3, (m % 4) * 128) if m < 4 else (4, 0)
                        psq, pqb = self.ps[bank], self.psb[bank]
                        p.op("scalar", lambda e, m=m, psq=psq, qoff=qoff: e.activation(
                            out=PT[:, m, :], in_=psq[0:128, qoff:qoff + 128], func=AF.Exp),
                            reads=[pqb], writes=[PTb], label=f"{tag}:b{b}t{t}h{h}:ex{m}")
                    if t >= 4:
                        p.op("vector", lambda e: e.tensor_tensor(
                            out=PT[:, 0, :], in0=PT[:, 0, :], in1=mlo[:], op=ALU.mult),
                            reads=[PTb, mlob], writes=[PTb], label=f"{tag}:b{b}t{t}h{h}:ml")
                    p.op("vector", lambda e: e.tensor_tensor(
                        out=PT[:, 4, :], in0=PT[:, 4, :], in1=mhi[:], op=ALU.mult),
                        reads=[PTb, mhib], writes=[PTb], label=f"{tag}:b{b}t{t}h{h}:mh")
                    pso, pob = self.ps[5], self.psb[5]
                    makes = [self.mm(pso[0:128, 0:65], PT[:, m, :],
                                     vroll[:, (t - 4 + m) % 5, h, :],
                                     start=(m == mfirst), stop=(m == 4))
                             for m in range(mfirst, 5)]
                    p.group("tensor", makes, reads=[PTb, vrollb], writes=[pob],
                            label=f"{tag}:b{b}t{t}h{h}:pv")
                    p.op("vector", lambda e, h=h: e.reciprocal(
                        out=rec[:, h:h + 1], in_=pso[0:128, 64:65]),
                        reads=[pob], writes=[recb], label=f"{tag}:b{b}t{t}h{h}:rc")
                    p.op("vector", lambda e, h=h: e.tensor_scalar(
                        out=osb[:, h * 64:(h + 1) * 64], in0=pso[0:128, 0:64],
                        scalar1=rec[:, h:h + 1], scalar2=None, op0=ALU.mult),
                        reads=[pob, recb], writes=[osbb], label=f"{tag}:b{b}t{t}h{h}:on")
                items = [(osb[:, i * 128:(i + 1) * 128],
                          128, 128, (oT[:, i, :], oTb), [osbb]) for i in range(4)]
                self.transpose_many(items, 0, 1, f"{tag}:b{b}t{t}:oT")
                p.dma("sync", ytile[:], res_d[b, t * 128:(t + 1) * 128, :],
                      reads=[residb], writes=[ytileb], label=f"{tag}:b{b}t{t}:ldy")
                for c in range(2):
                    psw, pwb = self.ps[6 + c], self.psb[6 + c]
                    makes = [self.mm(psw[0:128, 0:512], oT[:, k, :],
                                     wo_sb[:, k, c * 512:(c + 1) * 512],
                                     start=(k == 0), stop=(k == 3)) for k in range(4)]
                    p.group("tensor", makes, reads=[oTb, wob], writes=[pwb],
                            label=f"{tag}:b{b}t{t}:wo{c}")
                    p.op("vector", lambda e, c=c, psw=psw: e.tensor_add(
                        out=res[:, c * 512:(c + 1) * 512], in0=psw[0:128, 0:512],
                        in1=ytile[:, c * 512:(c + 1) * 512]),
                        reads=[pwb, ytileb], writes=[resb], label=f"{tag}:b{b}t{t}:rv{c}")
                p.dma("gpsimd", dst_d[b, t * 128:(t + 1) * 128, :], res[:],
                      reads=[resb], writes=[dstb_], label=f"{tag}:b{b}t{t}:st")

    def ffn_mlp(self, l, src, dst, dstT, src_resid, final, tag):
        p, d, sba = self.p, self.dram, self.sba
        self.new_phase(tag)
        ones, onesb = self.sb["ones_row"], self.bufs["ones_row"]
        xin0, xin0b = self.sba_persist("xin0", (128, D))
        ln, lnb = self.sba_persist("ln", (128, D))
        lnT, lnTb = self.sba_persist("lnT", (128, 8 * 128))
        stat, statb = self.sba_persist("stat", (128, 40))
        res, resb = self.sba_persist("res", (128, D))
        hT, hTb = sba("hT", (128, 8, 1024))
        uT, uTb = sba("uT", (128, 4, 1024))
        oacc, oaccb = sba("oacc", (128, 8, 1024))
        w1a, w1ab = sba("w1a", (128, 8, 512))
        w1g, w1gb = sba("w1g", (128, 8, 512))
        w2u, w2ub = sba("w2u", (128, 4, 1024))
        gsb, gsbb = sba("gsb", (128, 512))
        usb, usbb = sba("usb", (128, 512))
        lng, lngb = sba("lng", (128, 8))
        lnb_, lnbb = sba("lnbb", (128, 8))
        b1r, b1rb = sba("b1r", (1, 1024))
        b2r, b2rb = sba("b2r", (1, 1024))
        xf2, xf2b = self.sba_persist("xin1", (128, D))
        p.dma("sync", lng[:], d["ln_g"][l].rearrange("(k p) -> p k", p=128),
              writes=[lngb], label=f"{tag}:ldg", slow_ok=True)
        p.dma("sync", lnb_[:], d["ln_b"][l].rearrange("(k p) -> p k", p=128),
              writes=[lnbb], label=f"{tag}:ldb", slow_ok=True)
        p.dma("sync", b2r[:], d["b2"][l].rearrange("(o c) -> o c", o=1),
              writes=[b2rb], label=f"{tag}:ldb2")
        UCW = [512, 512, 512, 512, 512, 170]
        UC0 = [0, 512, 1024, 1536, 2048, 2560]
        src_d, srcb = d[src], self.dbuf[src]
        dst_d, dstb_ = d[dst], self.dbuf[dst]
        for b in range(B_CORE):
            for hh in range(2):
                toff = hh * 1024
                for tt in range(8):
                    t0 = toff + tt * 128
                    p.dma("sync", xin0[:, 0:512], src_d[b, t0:t0 + 128, 0:512],
                          reads=[srcb], writes=[xin0b], label=f"{tag}:b{b}h{hh}t{tt}:l1")
                    if t0 == 0:
                        p.dma("sync", xin0[1:128, 512:1024], src_d[b, 0:127, 512:1024],
                              reads=[srcb], writes=[xin0b], label=f"{tag}:b{b}h{hh}t{tt}:l2")
                        p.op("vector", lambda e: e.memset(xin0[0:1, 512:1024], 0.0),
                             writes=[xin0b], label=f"{tag}:z")
                    else:
                        p.dma("sync", xin0[:, 512:1024], src_d[b, t0 - 1:t0 + 127, 512:1024],
                              reads=[srcb], writes=[xin0b], label=f"{tag}:b{b}h{hh}t{tt}:l2")
                    inv_ap, nb_ap = self.ln_stats(xin0[:], xin0b, stat, statb, 0,
                                                  f"{tag}:b{b}h{hh}t{tt}")
                    p.op("scalar", lambda e, inv_ap=inv_ap, nb_ap=nb_ap:
                         e.activation(out=ln[:], in_=xin0[:], func=AF.Identity,
                                      scale=inv_ap, bias=nb_ap),
                         reads=[xin0b, statb], writes=[lnb], label=f"{tag}:b{b}h{hh}t{tt}:ln")
                    for i in range(8):
                        bank = i % 2
                        ps_, pb_ = self.ps[bank], self.psb[bank]
                        p.op("tensor", lambda e, i=i, ps_=ps_: e.transpose(
                            out=ps_[0:128, 0:128], in_=ln[:, i * 128:(i + 1) * 128],
                            identity=self.sb["identity"][:]),
                            reads=[lnb, self.bufs["identity"]], writes=[pb_],
                            label=f"{tag}:b{b}h{hh}t{tt}:T{i}")
                        p.op("scalar", lambda e, i=i, tt=tt, ps_=ps_: e.activation(
                            out=hT[:, i, tt * 128:(tt + 1) * 128], in_=ps_[0:128, 0:128],
                            func=AF.Identity, scale=lng[:, i:i + 1], bias=lnb_[:, i:i + 1]),
                            reads=[pb_, lngb, lnbb], writes=[hTb],
                            label=f"{tag}:b{b}h{hh}t{tt}:A{i}")
                for uc in range(6):
                    w = UCW[uc]
                    u0 = UC0[uc]
                    nks = (w + 127) // 128
                    p.dma("sync", w1a[:, :, 0:w],
                          d["w1"][l].rearrange("(k p) c -> p k c", p=128)[:, :, u0:u0 + w],
                          writes=[w1ab], label=f"{tag}:b{b}h{hh}u{uc}:lwa")
                    p.dma("sync", w1g[:, :, 0:w],
                          d["w1"][l].rearrange("(k p) c -> p k c", p=128)[:, :, 2730 + u0:2730 + u0 + w],
                          writes=[w1gb], label=f"{tag}:b{b}h{hh}u{uc}:lwg")
                    if w == 512:
                        p.dma("sync", w2u[:, 0:4, :],
                              d["w2"][l][u0:u0 + 512].rearrange("(a p) c -> p a c", p=128),
                              writes=[w2ub], label=f"{tag}:b{b}h{hh}u{uc}:lw2")
                    else:
                        p.dma("sync", w2u[:, 0, :], d["w2"][l][u0:u0 + 128],
                              writes=[w2ub], label=f"{tag}:b{b}h{hh}u{uc}:lw2a")
                        p.dma("sync", w2u[0:42, 1, :], d["w2"][l][u0 + 128:u0 + 170],
                              writes=[w2ub], label=f"{tag}:b{b}h{hh}u{uc}:lw2b")
                    p.dma("sync", b1r[0:1, 0:w], d["b1"][l].rearrange("(o c) -> o c", o=1)[:, u0:u0 + w],
                          writes=[b1rb], label=f"{tag}:b{b}h{hh}u{uc}:lb1a")
                    p.dma("sync", b1r[0:1, 512:512 + w],
                          d["b1"][l].rearrange("(o c) -> o c", o=1)[:, 2730 + u0:2730 + u0 + w],
                          writes=[b1rb], label=f"{tag}:b{b}h{hh}u{uc}:lb1g")
                    for tt in range(8):
                        psA, pAb = self.ps[2], self.psb[2]
                        psG, pGb = self.ps[3], self.psb[3]
                        makes = [self.mm(psA[0:128, 0:w], hT[:, k, tt * 128:(tt + 1) * 128],
                                         w1a[:, k, 0:w], start=(k == 0), stop=False)
                                 for k in range(8)]
                        makes.append(self.mm(psA[0:128, 0:w], ones[:], b1r[0:1, 0:w],
                                             start=False, stop=True))
                        p.group("tensor", makes, reads=[hTb, w1ab, onesb, b1rb],
                                writes=[pAb], label=f"{tag}:b{b}h{hh}u{uc}t{tt}:a")
                        makes = [self.mm(psG[0:128, 0:w], hT[:, k, tt * 128:(tt + 1) * 128],
                                         w1g[:, k, 0:w], start=(k == 0), stop=False)
                                 for k in range(8)]
                        makes.append(self.mm(psG[0:128, 0:w], ones[:], b1r[0:1, 512:512 + w],
                                             start=False, stop=True))
                        p.group("tensor", makes, reads=[hTb, w1gb, onesb, b1rb],
                                writes=[pGb], label=f"{tag}:b{b}h{hh}u{uc}t{tt}:g")
                        p.op("scalar", lambda e, w=w: e.activation(
                            out=gsb[:, 0:w], in_=psG[0:128, 0:w], func=AF.Gelu),
                            reads=[pGb], writes=[gsbb], label=f"{tag}:b{b}h{hh}u{uc}t{tt}:gl")
                        p.op("vector", lambda e, w=w: e.tensor_tensor(
                            out=usb[:, 0:w], in0=psA[0:128, 0:w], in1=gsb[:, 0:w],
                            op=ALU.mult),
                            reads=[pAb, gsbb], writes=[usbb], label=f"{tag}:b{b}h{hh}u{uc}t{tt}:u")
                        items = []
                        for ks in range(nks):
                            wk = min(128, w - ks * 128)
                            items.append((usb[:, ks * 128:ks * 128 + wk], 128, wk,
                                          (uT[0:wk, ks, tt * 128:(tt + 1) * 128], uTb),
                                          [usbb]))
                        self.transpose_many(items, 0, 1, f"{tag}:b{b}h{hh}u{uc}t{tt}:uT")
                    for tt in range(8):
                        for c in range(2):
                            psW, pWb = self.ps[4 + c], self.psb[4 + c]
                            makes = []
                            for ks in range(nks):
                                wk = min(128, w - ks * 128)
                                makes.append(self.mm(
                                    psW[0:128, 0:512],
                                    uT[0:wk, ks, tt * 128:(tt + 1) * 128],
                                    w2u[0:wk, ks, c * 512:(c + 1) * 512],
                                    start=(ks == 0), stop=(ks == nks - 1 and uc != 0)))
                            if uc == 0:
                                makes.append(self.mm(psW[0:128, 0:512], ones[:],
                                                     b2r[0:1, c * 512:(c + 1) * 512],
                                                     start=False, stop=True))
                            p.group("tensor", makes, reads=[uTb, w2ub, onesb, b2rb],
                                    writes=[pWb], label=f"{tag}:b{b}h{hh}u{uc}t{tt}:w2{c}")
                            if uc == 0:
                                p.op("vector", lambda e, tt=tt, c=c, psW=psW: e.tensor_copy(
                                    out=oacc[:, tt, c * 512:(c + 1) * 512],
                                    in_=psW[0:128, 0:512]),
                                    reads=[pWb], writes=[oaccb],
                                    label=f"{tag}:b{b}h{hh}u{uc}t{tt}:oc{c}")
                            else:
                                p.op("vector", lambda e, tt=tt, c=c, psW=psW: e.tensor_add(
                                    out=oacc[:, tt, c * 512:(c + 1) * 512],
                                    in0=oacc[:, tt, c * 512:(c + 1) * 512],
                                    in1=psW[0:128, 0:512]),
                                    reads=[pWb, oaccb], writes=[oaccb],
                                    label=f"{tag}:b{b}h{hh}u{uc}t{tt}:oa{c}")
                for tt in range(8):
                    t0 = toff + tt * 128
                    p.dma("sync", xin0[:], src_d[b, t0:t0 + 128, :],
                          reads=[srcb], writes=[xin0b], label=f"{tag}:b{b}h{hh}t{tt}:lx")
                    p.op("vector", lambda e, tt=tt: e.tensor_add(
                        out=res[:], in0=oacc[:, tt, :], in1=xin0[:]),
                        reads=[oaccb, xin0b], writes=[resb], label=f"{tag}:b{b}h{hh}t{tt}:r1")
                    if src_resid is not None:
                        p.dma("sync", xf2[:], d[src_resid][b, t0:t0 + 128, :],
                              reads=[self.dbuf[src_resid]], writes=[xf2b],
                              label=f"{tag}:b{b}h{hh}t{tt}:lr")
                        p.op("vector", lambda e: e.tensor_add(
                            out=res[:], in0=res[:], in1=xf2[:]),
                            reads=[resb, xf2b], writes=[resb],
                            label=f"{tag}:b{b}h{hh}t{tt}:r2")
                    p.dma("gpsimd", dst_d[b, t0:t0 + 128, :], res[:],
                          reads=[resb], writes=[dstb_], label=f"{tag}:b{b}h{hh}t{tt}:st")
                    if dstT is not None:
                        items = [(res[:, i * 128:(i + 1) * 128], 128, 128,
                                  (lnT[:, i * 128:(i + 1) * 128], lnTb), [resb])
                                 for i in range(8)]
                        self.transpose_many(items, 6, 7, f"{tag}:b{b}h{hh}t{tt}:rT")
                        for kc in range(8):
                            p.dma("gpsimd",
                                  d[dstT][b, kc * 128:(kc + 1) * 128, t0:t0 + 128],
                                  lnT[:, kc * 128:(kc + 1) * 128],
                                  reads=[lnTb], writes=[self.dbuf[dstT]],
                                  label=f"{tag}:b{b}h{hh}t{tt}:sT{kc}")

    def tp_qkvmin(self):
        p, d, sba = self.p, self.dram, self.sba
        self.ln_transpose_phase("x", "Y2", "Y2T", "m1")
        self.new_phase("m2")
        wq, wqb = sba("wq", (128, 8, 1536))
        p.dma("sync", wq[:], d["wqkv"][0].rearrange("(k p) c -> p k c", p=128),
              writes=[wqb], label="m2:ldw")
        STAGE = getattr(self, "dbg_stage", 0)
        if STAGE >= 1:
            rcos, rcosb = sba("rcos", (128, 16, 64))
            rsin, rsinb = sba("rsin", (128, 16, 64))
            rsc, rscb = sba("rsc", (128, 16, 64))
            risc, riscb = sba("risc", (128, 16, 64))
            for nm, (tt, bb_) in (("rot_cos", (rcos, rcosb)), ("rot_sin", (rsin, rsinb)),
                                  ("rot_scale", (rsc, rscb)), ("rot_iscale", (risc, riscb))):
                p.dma("sync", tt[:], d[nm].rearrange("(t p) c -> p t c", p=128),
                      writes=[bb_], label=f"m2:ld_{nm}")
        if STAGE >= 2:
            qs, qsb = sba("qs", (128, 64))
            ks, ksb = sba("ks", (128, 64))
            p.dma("sync", qs[:], d["q_scale"][0].rearrange("(o c) -> o c", o=1).broadcast_to((128, 64)),
                  writes=[qsb], label="m2:ld_qs")
            p.dma("sync", ks[:], d["k_scale"][0].rearrange("(o c) -> o c", o=1).broadcast_to((128, 64)),
                  writes=[ksb], label="m2:ld_ks")
            qsr, qsrb = sba("qsr", (128, 64))
            ksr, ksrb = sba("ksr", (128, 64))
            p.dma("sync", qsr[:, 0:32], d["q_scale"][0].rearrange("(o c) -> o c", o=1)[:, 32:64].broadcast_to((128, 32)),
                  writes=[qsrb], label="m2:ld_qsr1")
            p.dma("sync", qsr[:, 32:64], d["q_scale"][0].rearrange("(o c) -> o c", o=1)[:, 0:32].broadcast_to((128, 32)),
                  writes=[qsrb], label="m2:ld_qsr2")
            p.dma("sync", ksr[:, 0:32], d["k_scale"][0].rearrange("(o c) -> o c", o=1)[:, 32:64].broadcast_to((128, 32)),
                  writes=[ksrb], label="m2:ld_ksr1")
            p.dma("sync", ksr[:, 32:64], d["k_scale"][0].rearrange("(o c) -> o c", o=1)[:, 0:32].broadcast_to((128, 32)),
                  writes=[ksrb], label="m2:ld_ksr2")
        if STAGE >= 3:
            cqq, cqqb = sba("cqq", (128, 16, 64))
            sqq, sqqb = sba("sqq", (128, 16, 64))
            ckk, ckkb = sba("ckk", (128, 16, 64))
            skk, skkb = sba("skk", (128, 16, 64))
            tmp, tmpb = sba("ttmp", (128, 16, 64))
            for (srcA, srcAb, srcS, srcSb, scl, sclb, fac, dst, dstb, nm) in (
                    (rcos, rcosb, rsc, rscb, qs, qsb, 8.0, cqq, cqqb, "cqq"),
                    (rsin, rsinb, rsc, rscb, qsr, qsrb, 8.0, sqq, sqqb, "sqq"),
                    (rcos, rcosb, risc, riscb, ks, ksb, 1.0, ckk, ckkb, "ckk"),
                    (rsin, rsinb, risc, riscb, ksr, ksrb, 1.0, skk, skkb, "skk")):
                p.op("vector", lambda e, a=srcA, s=srcS: e.tensor_tensor(
                    out=tmp[:], in0=a[:], in1=s[:], op=ALU.mult),
                    reads=[srcAb, srcSb], writes=[tmpb], label=f"m2:{nm}1")
                p.op("vector", lambda e, scl=scl, fac=fac, dst=dst: e.scalar_tensor_tensor(
                    out=dst[:], in0=tmp[:], scalar=fac,
                    in1=scl[:].rearrange("p (a c) -> p a c", a=1).broadcast_to((128, 16, 64)),
                    op0=ALU.mult, op1=ALU.mult),
                    reads=[tmpb, sclb], writes=[dstb], label=f"m2:{nm}2")
        ysh, yshb = sba("ysh", (128, 8, 128))
        vsb, vsbb = sba("vsb", (128, 512))
        qh, qhb = sba("qh", (128, 8, 64))
        kh, khb = sba("kh", (128, 8, 64))
        t1, t1b = sba("t1", (128, 8, 64))
        sst, sstb = sba("sst", (128, 48))
        r1, r1b = sba("r1", (128, 8, 32))
        r2, r2b = sba("r2", (128, 8, 32))
        inT_d, inTb = d["Y2T"], self.dbuf["Y2T"]
        qkv_d, qkvb = d["QKV"], self.dbuf["QKV"]
        for b in range(B_CORE):
            for t in range(NT):
                for kc in range(4):
                    p.dma("sync", ysh[:, kc, :],
                          inT_d[b, kc * 128:(kc + 1) * 128, t * 128:(t + 1) * 128],
                          reads=[inTb], writes=[yshb], label=f"m2:b{b}t{t}:lda{kc}")
                for kc in range(4, 8):
                    if t == 0:
                        p.dma("sync", ysh[:, kc, 1:128],
                              inT_d[b, kc * 128:(kc + 1) * 128, 0:127],
                              reads=[inTb], writes=[yshb], label=f"m2:b{b}t{t}:ldb{kc}")
                    else:
                        p.dma("sync", ysh[:, kc, :],
                              inT_d[b, kc * 128:(kc + 1) * 128, t * 128 - 1:(t + 1) * 128 - 1],
                              reads=[inTb], writes=[yshb], label=f"m2:b{b}t{t}:ldb{kc}")
                if t == 0:
                    p.op("vector", lambda e: e.memset(ysh[:, 4:8, 0:1], 0.0),
                         writes=[yshb], label=f"m2:b{b}t{t}:z")
                if b == 0 and t == 2:
                    p.dma("gpsimd", d["dbg_lnT"], ysh[:].rearrange("p a t -> p (a t)"),
                          reads=[yshb], writes=[self.dbuf["dbg_lnT"]], label="m2:dbg")
                ncblk = 3 if STAGE >= 4 else 1
                for c in ([2] if STAGE < 4 else [0, 1, 2]):
                    psq, pqb = self.ps[c], self.psb[c]
                    makes = [self.mm(psq[0:128, 0:512], ysh[:, k, :],
                                     wq[:, k, c * 512:(c + 1) * 512],
                                     start=(k == 0), stop=(k == 7)) for k in range(8)]
                    p.group("tensor", makes, reads=[yshb, wqb], writes=[pqb],
                            label=f"m2:b{b}t{t}:mm{c}")
                psq, pqb = self.ps[2], self.psb[2]
                p.op("scalar", lambda e: e.activation(out=vsb[:], in_=psq[0:128, 0:512],
                                                      func=AF.Identity),
                     reads=[pqb], writes=[vsbb], label=f"m2:b{b}t{t}:vev")
                p.dma("gpsimd", qkv_d[b, t * 128:(t + 1) * 128, 1024:1536], vsb[:],
                      reads=[vsbb], writes=[qkvb], label=f"m2:b{b}t{t}:vst")
                if STAGE >= 5:
                    for (ci, col) in ((0, 0), (1, 8)):
                        psq2, pqb2 = self.ps[ci], self.psb[ci]
                        p.op("scalar", lambda e, psq2=psq2: e.activation(
                            out=t1[:], in_=psq2[0:128, 0:512].rearrange("p (h c) -> p h c", h=8),
                            func=AF.Square),
                            reads=[pqb2], writes=[t1b], label=f"m2:b{b}t{t}:sq{ci}")
                        p.op("vector", lambda e, col=col: e.reduce_sum(
                            out=sst[:, col:col + 8].rearrange("p (h o) -> p h o", o=1),
                            in_=t1[:], axis=AX.X),
                            reads=[t1b], writes=[sstb], label=f"m2:b{b}t{t}:ssq{ci}")
                        p.op("scalar", lambda e, col=col: e.activation(
                            out=sst[:, col + 16:col + 24], in_=sst[:, col:col + 8], func=AF.Sqrt),
                            reads=[sstb], writes=[sstb], label=f"m2:b{b}t{t}:nrm{ci}")
                        p.op("vector", lambda e, col=col: e.tensor_scalar_max(
                            out=sst[:, col + 16:col + 24], in0=sst[:, col + 16:col + 24],
                            scalar1=1e-12),
                            reads=[sstb], writes=[sstb], label=f"m2:b{b}t{t}:clip{ci}")
                        p.op("vector", lambda e, col=col: e.reciprocal(
                            out=sst[:, col + 32:col + 40], in_=sst[:, col + 16:col + 24]),
                            reads=[sstb], writes=[sstb], label=f"m2:b{b}t{t}:rec{ci}")
                    for (ci, dst2, dstb2, col) in ((0, qh, qhb, 0), (1, kh, khb, 8)):
                        psq2, pqb2 = self.ps[ci], self.psb[ci]
                        p.op("vector", lambda e, psq2=psq2, dst2=dst2, col=col: e.tensor_tensor(
                            out=dst2[:], in0=psq2[0:128, 0:512].rearrange("p (h c) -> p h c", h=8),
                            in1=sst[:, col + 32:col + 40].rearrange("p (h o) -> p h o", o=1)
                                .broadcast_to((128, 8, 64)), op=ALU.mult),
                            reads=[pqb2, sstb], writes=[dstb2], label=f"m2:b{b}t{t}:nm{ci}")
                    if STAGE >= 6:
                        def rot2(src2, srcb2, ctab, ctabb, stab, stabb, t2, dst_cols, b2, nm):
                            p.op("vector", lambda e: e.tensor_tensor(
                                out=t1[:], in0=src2[:],
                                in1=ctab[:, t2:t2 + 1, :].broadcast_to((128, 8, 64)), op=ALU.mult),
                                reads=[srcb2, ctabb], writes=[t1b], label=f"{nm}:c")
                            p.op("vector", lambda e: e.tensor_tensor(
                                out=r1[:], in0=src2[:, :, 32:64],
                                in1=stab[:, t2:t2 + 1, 0:32].broadcast_to((128, 8, 32)), op=ALU.mult),
                                reads=[srcb2, stabb], writes=[r1b], label=f"{nm}:r1")
                            p.op("vector", lambda e: e.tensor_tensor(
                                out=r2[:], in0=src2[:, :, 0:32],
                                in1=stab[:, t2:t2 + 1, 32:64].broadcast_to((128, 8, 32)), op=ALU.mult),
                                reads=[srcb2, stabb], writes=[r2b], label=f"{nm}:r2")
                            p.op("vector", lambda e: e.tensor_sub(
                                out=t1[:, :, 0:32], in0=t1[:, :, 0:32], in1=r1[:]),
                                reads=[t1b, r1b], writes=[t1b], label=f"{nm}:s1")
                            p.op("vector", lambda e: e.tensor_add(
                                out=t1[:, :, 32:64], in0=t1[:, :, 32:64], in1=r2[:]),
                                reads=[t1b, r2b], writes=[t1b], label=f"{nm}:s2")
                            p.dma("gpsimd", qkv_d[b2, t2 * 128:(t2 + 1) * 128, dst_cols[0]:dst_cols[1]],
                                  t1[:], reads=[t1b], writes=[qkvb], label=f"{nm}:st")
                        rot2(qh, qhb, cqq, cqqb, sqq, sqqb, t, (0, 512), b, f"m2:b{b}t{t}:rq")
                        rot2(kh, khb, ckk, ckkb, skk, skkb, t, (512, 1024), b, f"m2:b{b}t{t}:rk")
                    else:
                        p.dma("gpsimd", qkv_d[b, t * 128:(t + 1) * 128, 0:512],
                              qh[:].rearrange("p a c -> p (a c)"),
                              reads=[qhb], writes=[qkvb], label=f"m2:b{b}t{t}:sq")
                        p.dma("gpsimd", qkv_d[b, t * 128:(t + 1) * 128, 512:1024],
                              kh[:].rearrange("p a c -> p (a c)"),
                              reads=[khb], writes=[qkvb], label=f"m2:b{b}t{t}:sk")

    def gla_block(self, src, dst, final, tag):
        p, d, sba = self.p, self.dram, self.sba
        ones, onesb = self.sb["ones_row"], self.bufs["ones_row"]
        self.ln_transpose_phase(src, None, "X3T", f"{tag}G1")
        self.new_phase(f"{tag}G1b")
        wstage, wstageb = sba("g1b_wstage", (128, 128))
        w1T, w1Tb = sba("g1b_w1T", (16, 1024))
        wgk2, wgk2b = sba("wgk2", (16, 1024))
        w12, w12b = sba("w12", (128, 8, 1024))
        bgk, bgkb = sba("bgk", (128, 8))
        p.dma("sync", wstage[:, 0:8 * 16].rearrange("p (c s) -> p c s", c=8),
              d["Wgk1"].rearrange("(c p) s -> p c s", p=128),
              writes=[wstageb], label=f"{tag}G1b:ldw1")
        p.dma("sync", wgk2[:], d["Wgk2"], writes=[wgk2b], label=f"{tag}G1b:ldw2")
        p.dma("sync", bgk[:], d["bgk2"].rearrange("(k p) -> p k", p=128),
              writes=[bgkb], label=f"{tag}G1b:ldbgk", slow_ok=True)
        items = [(wstage[:, c * 16:(c + 1) * 16], 128, 16,
                  (w1T[0:16, c * 128:(c + 1) * 128], w1Tb), [wstageb])
                 for c in range(8)]
        self.transpose_many(items, 0, 1, f"{tag}G1b:T")
        for fc in range(8):
            for cb in range(2):
                psq, pqb = self.ps[2 + (cb % 2)], self.psb[2 + (cb % 2)]
                p.op("tensor", self.mm(psq[0:128, 0:512],
                                       w1T[0:16, fc * 128:(fc + 1) * 128],
                                       wgk2[0:16, cb * 512:(cb + 1) * 512],
                                       start=True, stop=True),
                     reads=[w1Tb, wgk2b], writes=[pqb], label=f"{tag}G1b:mm{fc}.{cb}")
                p.op("vector", lambda e, fc=fc, cb=cb, psq=psq: e.tensor_copy(
                    out=w12[:, fc, cb * 512:(cb + 1) * 512], in_=psq[0:128, 0:512]),
                    reads=[pqb], writes=[w12b], label=f"{tag}G1b:ev{fc}.{cb}")
        x3T_d, x3Tb = d["X3T"], self.dbuf["X3T"]
        gk_d, gkb = d["GK"], self.dbuf["GK"]
        wg_a, wg_ab = sba("wg_a", (128, 8, 1024))
        wg_b, wg_bb = sba("wg_b", (128, 8, 1024))
        p.dma("sync", wg_a[:], d["Wg"].rearrange("(k p) c -> p k c", p=128)[:, :, 0:1024],
              writes=[wg_ab], label=f"{tag}G2:lwa")
        p.dma("sync", wg_b[:], d["Wg"].rearrange("(k p) c -> p k c", p=128)[:, :, 1024:2048],
              writes=[wg_bb], label=f"{tag}G2:lwb")
        x3g2, x3g2b = sba("x3g2", (128, 8, 128))
        gtile, gtileb = sba("gtile", (128, 512))
        gksl, gkslb = sba("gksl", (128, 8, 128))
        gt_d, gtb = d["GT"], self.dbuf["GT"]
        for b in range(B_CORE):
            for t in range(NT):
                for kc in range(8):
                    p.dma("sync", x3g2[:, kc, :],
                          x3T_d[b, kc * 128:(kc + 1) * 128, t * 128:(t + 1) * 128],
                          reads=[x3Tb], writes=[x3g2b], label=f"{tag}G2:b{b}t{t}:l{kc}")
                for c in range(4):
                    wsrc, wsrcb = (wg_a, wg_ab) if c < 2 else (wg_b, wg_bb)
                    coff = (c % 2) * 512
                    psq, pqb = self.ps[c % 2], self.psb[c % 2]
                    makes = [self.mm(psq[0:128, 0:512], x3g2[:, k, :],
                                     wsrc[:, k, coff:coff + 512],
                                     start=(k == 0), stop=(k == 7)) for k in range(8)]
                    p.group("tensor", makes, reads=[x3g2b, wsrcb], writes=[pqb],
                            label=f"{tag}G2:b{b}t{t}:mm{c}")
                    p.op("scalar", lambda e, psq=psq: e.activation(
                        out=gtile[:], in_=psq[0:128, 0:512], func=AF.Silu),
                        reads=[pqb], writes=[gtileb], label=f"{tag}G2:b{b}t{t}:si{c}")
                    p.dma("gpsimd", gt_d[b, t * 128:(t + 1) * 128, c * 512:(c + 1) * 512],
                          gtile[:], reads=[gtileb], writes=[gtb],
                          label=f"{tag}G2:b{b}t{t}:st{c}")
                for dc8 in range(8):
                    psq, pqb = self.ps[2 + (dc8 % 2)], self.psb[2 + (dc8 % 2)]
                    makes = [self.mm(psq[0:128, 0:128],
                                     w12[:, k, dc8 * 128:(dc8 + 1) * 128],
                                     x3g2[:, k, :], start=(k == 0), stop=(k == 7))
                             for k in range(8)]
                    p.group("tensor", makes, reads=[w12b, x3g2b], writes=[pqb],
                            label=f"{tag}G2:b{b}t{t}:gk{dc8}")
                    p.op("scalar", lambda e, dc8=dc8, psq=psq: e.activation(
                        out=gksl[:, dc8, :], in_=psq[0:128, 0:128], func=AF.Sigmoid,
                        bias=bgk[:, dc8:dc8 + 1]),
                        reads=[pqb, bgkb], writes=[gkslb],
                        label=f"{tag}G2:b{b}t{t}:gs{dc8}")
                for dc8 in range(8):
                    p.dma("gpsimd", gk_d[b, dc8 * 128:(dc8 + 1) * 128, t * 128:(t + 1) * 128],
                          gksl[:, dc8, :], reads=[gkslb], writes=[gkb],
                          label=f"{tag}G2:b{b}t{t}:gst{dc8}")
                for dc8 in range(8):
                    psq, pqb = self.ps[2 + (dc8 % 2)], self.psb[2 + (dc8 % 2)]
                    makes = [self.mm(psq[0:128, 0:128],
                                     w12[:, k, dc8 * 128:(dc8 + 1) * 128],
                                     x3g2[:, k, :], start=(k == 0), stop=(k == 7))
                             for k in range(8)]
                    p.group("tensor", makes, reads=[w12b, x3g2b], writes=[pqb],
                            label=f"{tag}G2:b{b}t{t}:gk{dc8}")
                    p.op("scalar", lambda e, dc8=dc8, psq=psq: e.activation(
                        out=gksl[:, dc8, :], in_=psq[0:128, 0:128], func=AF.Sigmoid,
                        bias=bgk[:, dc8:dc8 + 1]),
                        reads=[pqb, bgkb], writes=[gkslb],
                        label=f"{tag}G2:b{b}t{t}:gs{dc8}")
                for dc8 in range(8):
                    p.dma("gpsimd", gk_d[b, dc8 * 128:(dc8 + 1) * 128, t * 128:(t + 1) * 128],
                          gksl[:, dc8, :], reads=[gkslb], writes=[gkb],
                          label=f"{tag}G2:b{b}t{t}:gst{dc8}")
        self.new_phase(f"{tag}G3")
        v_tm, v_tmb = sba("v_tm", (128, 16, 512))
        q_dm, q_dmb = sba("q_dm", (128, 2, T))
        k_dm, k_dmb = sba("k_dm", (128, 2, T))
        gkr, gkrb = sba("gkr", (128, 2, T))
        gcum, gcumb = sba("gcum", (128, 2, T))
        x3g3, x3g3b = sba("x3g3", (128, 8, 512))
        wqh, wqhb = sba("wqh", (128, 8, 256))
        wkh, wkhb = sba("wkh", (128, 8, 256))
        wvh, wvhb = sba("wvh", (128, 8, 512))
        rstm, rstmb = sba("rstm", (128, T))
        p.dma("sync", rstm[:], d["gla_reset"].broadcast_to((128, T)),
              writes=[rstmb], label=f"{tag}G3:ldrst")
        mhi2, mhi2b = sba("mhi2", (128, 128))
        p.dma("sync", mhi2[:], d["mask_hi"], writes=[mhi2b], label=f"{tag}G3:ldmh")
        S_sb, S_sbb = sba("S_sb", (128, 2, 512))
        egl, eglb = sba("egl", (128, 64))
        kgc, kgcb = sba("kgc", (128, 2, 64))
        kgT, kgTb = sba("kgT", (64, 256))
        ATs, ATsb = sba("ATs", (128, 64))
        ovb, ovbb = sba("ovb", (64, 512))
        vsh, vshb = sba("vsh", (64, 512))
        go_d, gob = d["GO"], self.dbuf["GO"]
        eg = x3g3
        egb = x3g3b
        for b in range(B_CORE):
            for h in range(4):
                p.dma("sync", wqh[:], d["Wq"].rearrange("(k p) c -> p k c", p=128)
                      [:, :, h * 256:(h + 1) * 256], writes=[wqhb], label=f"{tag}G3:b{b}h{h}:lwq")
                p.dma("sync", wkh[:], d["Wk"].rearrange("(k p) c -> p k c", p=128)
                      [:, :, h * 256:(h + 1) * 256], writes=[wkhb], label=f"{tag}G3:b{b}h{h}:lwk")
                p.dma("sync", wvh[:], d["Wv"].rearrange("(k p) c -> p k c", p=128)
                      [:, :, h * 512:(h + 1) * 512], writes=[wvhb], label=f"{tag}G3:b{b}h{h}:lwv")
                for tg in range(4):
                    for kc in range(8):
                        p.dma("sync", x3g3[:, kc, :],
                              x3T_d[b, kc * 128:(kc + 1) * 128, tg * 512:(tg + 1) * 512],
                              reads=[x3Tb], writes=[x3g3b],
                              label=f"{tag}G3:b{b}h{h}g{tg}:l{kc}")
                    for dc in range(2):
                        psq, pqb = self.ps[dc], self.psb[dc]
                        makes = [self.mm(psq[0:128, 0:512],
                                         wqh[:, k, dc * 128:(dc + 1) * 128],
                                         x3g3[:, k, :], start=(k == 0), stop=(k == 7))
                                 for k in range(8)]
                        p.group("tensor", makes, reads=[wqhb, x3g3b], writes=[pqb],
                                label=f"{tag}G3:b{b}h{h}g{tg}:q{dc}")
                        p.op("vector", lambda e, dc=dc, tg=tg, psq=psq: e.tensor_copy(
                            out=q_dm[:, dc, tg * 512:(tg + 1) * 512], in_=psq[0:128, 0:512]),
                            reads=[pqb], writes=[q_dmb], label=f"{tag}G3:b{b}h{h}g{tg}:qe{dc}")
                        psq, pqb = self.ps[2 + dc], self.psb[2 + dc]
                        makes = [self.mm(psq[0:128, 0:512],
                                         wkh[:, k, dc * 128:(dc + 1) * 128],
                                         x3g3[:, k, :], start=(k == 0), stop=(k == 7))
                                 for k in range(8)]
                        p.group("tensor", makes, reads=[wkhb, x3g3b], writes=[pqb],
                                label=f"{tag}G3:b{b}h{h}g{tg}:k{dc}")
                        p.op("vector", lambda e, dc=dc, tg=tg, psq=psq: e.tensor_copy(
                            out=k_dm[:, dc, tg * 512:(tg + 1) * 512], in_=psq[0:128, 0:512]),
                            reads=[pqb], writes=[k_dmb], label=f"{tag}G3:b{b}h{h}g{tg}:ke{dc}")
                    for tt in range(4):
                        psq, pqb = self.ps[4 + (tt % 2)], self.psb[4 + (tt % 2)]
                        makes = [self.mm(psq[0:128, 0:512],
                                         x3g3[:, k, tt * 128:(tt + 1) * 128],
                                         wvh[:, k, :], start=(k == 0), stop=(k == 7))
                                 for k in range(8)]
                        p.group("tensor", makes, reads=[x3g3b, wvhb], writes=[pqb],
                                label=f"{tag}G3:b{b}h{h}g{tg}:v{tt}")
                        p.op("vector", lambda e, tg=tg, tt=tt, psq=psq: e.tensor_copy(
                            out=v_tm[:, tg * 4 + tt, :], in_=psq[0:128, 0:512]),
                            reads=[pqb], writes=[v_tmb], label=f"{tag}G3:b{b}h{h}g{tg}:ve{tt}")
                for dc in range(2):
                    p.dma("sync", gkr[:, dc, :],
                          gk_d[b, h * 256 + dc * 128: h * 256 + (dc + 1) * 128, :],
                          reads=[gkb], writes=[gkrb], label=f"{tag}G3:b{b}h{h}:lgk{dc}")
                p.op("scalar", lambda e: e.activation(out=gkr[:], in_=gkr[:], func=AF.Ln),
                     reads=[gkrb], writes=[gkrb], label=f"{tag}G3:b{b}h{h}:ln")
                p.op("vector", lambda e: e.tensor_copy(out=gcum[:], in_=gkr[:]),
                     reads=[gkrb], writes=[gcumb], label=f"{tag}G3:b{b}h{h}:gcp")
                p.op("scalar", lambda e: e.activation(
                    out=egl[:].rearrange("p (a c) -> p a c", a=2),
                    in_=gcum[:, :, 63::64], func=AF.Exp, scale=1.0 / 16),
                    reads=[gcumb], writes=[eglb], label=f"{tag}G3:b{b}h{h}:egl")
                p.op("scalar", lambda e: e.activation(out=eg[:].rearrange("p a c -> p (a c)")[:, 0:2 * T]
                                                      .rearrange("p (a c) -> p a c", a=2),
                                                      in_=gcum[:], func=AF.Exp, scale=1.0 / 16),
                     reads=[gcumb], writes=[egb], label=f"{tag}G3:b{b}h{h}:egp")
                p.op("vector", lambda e: e.scalar_tensor_tensor(
                    out=q_dm[:], in0=q_dm[:], scalar=1.0 / 16,
                    in1=eg[:].rearrange("p a c -> p (a c)")[:, 0:2 * T]
                    .rearrange("p (a c) -> p a c", a=2), op0=ALU.mult, op1=ALU.mult),
                    reads=[q_dmb, egb], writes=[q_dmb], label=f"{tag}G3:b{b}h{h}:qg")
                p.op("scalar", lambda e: e.activation(out=eg[:].rearrange("p a c -> p (a c)")[:, 0:2 * T]
                                                      .rearrange("p (a c) -> p a c", a=2),
                                                      in_=gcum[:], func=AF.Exp, scale=-1.0 / 16),
                     reads=[gcumb, q_dmb], writes=[egb], label=f"{tag}G3:b{b}h{h}:egn")
                p.op("vector", lambda e: e.tensor_tensor(
                    out=k_dm[:], in0=k_dm[:],
                    in1=eg[:].rearrange("p a c -> p (a c)")[:, 0:2 * T]
                    .rearrange("p (a c) -> p a c", a=2), op=ALU.mult),
                    reads=[k_dmb, egb], writes=[k_dmb], label=f"{tag}G3:b{b}h{h}:kx")
                p.op("vector", lambda e: e.memset(S_sb[:], 0.0),
                     writes=[S_sbb], label=f"{tag}G3:b{b}h{h}:s0")
                for n in range(32):
                    odd = n % 2
                    cr = slice(n * 64, (n + 1) * 64)
                    if odd:
                        p.dma("sync", vsh[0:64, :], v_tm[64:128, n // 2, :],
                              reads=[v_tmb], writes=[vshb],
                              label=f"{tag}G3:b{b}h{h}n{n}:vs")
                        vchunk, vchunkb = vsh[0:64, :], vshb
                    else:
                        vchunk, vchunkb = v_tm[0:64, n // 2, :], v_tmb
                    psA, pAb = self.ps[0], self.psb[0]
                    makes = [
                        (lambda e, dc=dc, cr=cr: e.matmul(
                            psA[0:64, 0:64], k_dm[:, dc, cr], q_dm[:, dc, cr],
                            start=(dc == 0), stop=(dc == 1))) for dc in range(2)]
                    p.group("tensor", makes, reads=[k_dmb, q_dmb], writes=[pAb],
                            label=f"{tag}G3:b{b}h{h}n{n}:at")
                    p.op("vector", lambda e: e.tensor_tensor(
                        out=ATs[0:64, :], in0=psA[0:64, 0:64],
                        in1=mhi2[0:64, 0:64], op=ALU.mult),
                        reads=[pAb, mhi2b], writes=[ATsb], label=f"{tag}G3:b{b}h{h}n{n}:am")
                    for dc in range(2):
                        p.op("vector", lambda e, dc=dc, cr=cr, n=n: e.tensor_scalar(
                            out=kgc[:, dc, :], in0=k_dm[:, dc, cr],
                            scalar1=egl[:, dc * 32 + n:dc * 32 + n + 1], scalar2=None, op0=ALU.mult),
                            reads=[k_dmb, eglb], writes=[kgcb],
                            label=f"{tag}G3:b{b}h{h}n{n}:kg{dc}")
                    psT, pTb = self.ps[1], self.psb[1]
                    for dc in range(2):
                        p.op("tensor", lambda e, dc=dc: e.transpose(
                            out=psT[0:64, dc * 128:(dc + 1) * 128],
                            in_=kgc[:, dc, :], identity=self.sb["identity"][:]),
                            reads=[kgcb, self.bufs["identity"]], writes=[pTb],
                            label=f"{tag}G3:b{b}h{h}n{n}:kt{dc}")
                    p.op("vector", lambda e: e.tensor_copy(
                        out=kgT[0:64, :], in_=psT[0:64, 0:256]),
                        reads=[pTb], writes=[kgTb], label=f"{tag}G3:b{b}h{h}n{n}:kte")
                    psO, pOb = self.ps[2], self.psb[2]
                    makes = [
                        (lambda e, vchunk=vchunk: e.matmul(
                            psO[0:64, 0:512], ATs[0:64, :],
                            vchunk, start=True, stop=False)),
                        (lambda e, cr=cr: e.matmul(psO[0:64, 0:512], q_dm[:, 0, cr],
                                                   S_sb[:, 0, :], start=False, stop=False)),
                        (lambda e, cr=cr: e.matmul(psO[0:64, 0:512], q_dm[:, 1, cr],
                                                   S_sb[:, 1, :], start=False, stop=True)),
                    ]
                    p.group("tensor", makes, reads=[ATsb, vchunkb, q_dmb, S_sbb],
                            writes=[pOb], label=f"{tag}G3:b{b}h{h}n{n}:o")
                    p.op("vector", lambda e: e.tensor_copy(out=ovb[0:64, :],
                                                           in_=psO[0:64, 0:512]),
                         reads=[pOb], writes=[ovbb], label=f"{tag}G3:b{b}h{h}n{n}:oe")
                    p.dma("gpsimd", go_d[b, h, cr, :], ovb[0:64, :],
                          reads=[ovbb], writes=[gob], label=f"{tag}G3:b{b}h{h}n{n}:st")
                    for dc in range(2):
                        psS, pSb = self.ps[3 + dc], self.psb[3 + dc]
                        p.op("tensor", lambda e, dc=dc, vchunk=vchunk, psS=psS: e.matmul(
                            psS[0:128, 0:512], kgT[0:64, dc * 128:(dc + 1) * 128], vchunk,
                            start=True, stop=True),
                            reads=[kgTb, vchunkb], writes=[pSb],
                            label=f"{tag}G3:b{b}h{h}n{n}:su{dc}")
                        p.op("vector", lambda e, dc=dc, psS=psS, n=n: e.scalar_tensor_tensor(
                            out=S_sb[:, dc, :], in0=S_sb[:, dc, :],
                            scalar=egl[:, dc * 32 + n:dc * 32 + n + 1], in1=psS[0:128, 0:512],
                            op0=ALU.mult, op1=ALU.add),
                            reads=[S_sbb, eglb, pSb], writes=[S_sbb],
                            label=f"{tag}G3:b{b}h{h}n{n}:sa{dc}")
                    if b == 0 and h == 0 and n == 0 and self.dbg_scratch:
                        sgt2, sgt2b = self.sba_persist("sgtile_sa", (128, 512))
                        p.op("vector", lambda e: e.tensor_copy(out=sgt2[:], in_=self.ps[3][0:128, 0:512]),
                             reads=[self.psb[3]], writes=[sgt2b], label="dbgpsev")
                        p.dma("gpsimd", d["dbg_ps"], sgt2[:],
                              reads=[sgt2b], writes=[self.dbuf["dbg_ps"]], label="dbgps")
                        p.dma("gpsimd", d["dbg_ln"],
                              S_sb[:].rearrange("p a c -> p (a c)"),
                              reads=[S_sbb], writes=[self.dbuf["dbg_ln"]],
                              label="dbgS0")
                    if b == 0 and h == 0 and n == 0 and self.dbg_scratch:
                        p.dma("gpsimd", d["dbg_lnT"][:, 576:640], kgc[:, 0, :],
                              reads=[kgcb], writes=[self.dbuf["dbg_lnT"]],
                              label="dbgkgc")
                        p.dma("gpsimd", d["dbg_lnT"][0:64, 640:768], kgT[0:64, 0:128],
                              reads=[kgTb], writes=[self.dbuf["dbg_lnT"]],
                              label="dbgkgT")
                        p.dma("gpsimd", d["dbg_xp"][:, 0, 0:16], v_tm[:, 0, 0:16],
                              reads=[v_tmb], writes=[self.dbuf["dbg_xp"]],
                              label="dbgvtm")
                        p.dma("gpsimd", d["dbg_lnT"][:, 768:800], egl[:, 0:32],
                              reads=[eglb], writes=[self.dbuf["dbg_lnT"]],
                              label="dbgegl")
                    if b == 0 and h == 0 and n == 1 and self.dbg_scratch:
                        p.dma("gpsimd", d["dbg_lnT"][0:64, 0:512], vsh[0:64, :],
                              reads=[vshb], writes=[self.dbuf["dbg_lnT"]],
                              label="dbgv1")
                        p.dma("gpsimd", d["dbg_lnT"][0:64, 512:576], ATs[0:64, :],
                              reads=[ATsb], writes=[self.dbuf["dbg_lnT"]],
                              label="dbga1")
        self.new_phase(f"{tag}G4")
        res, resb = self.sba_persist("res", (128, D))
        stat, statb = self.sba_persist("stat", (128, 40))
        xin0, xin0b = self.sba_persist("xin0", (128, D))
        wo_a2, wo_a2b = sba("wo_a2", (128, 8, 1024))
        wo_b2, wo_b2b = sba("wo_b2", (128, 8, 1024))
        p.dma("sync", wo_a2[:], d["Wo"][0:1024].rearrange("(k p) c -> p k c", p=128),
              writes=[wo_a2b], label=f"{tag}G4:lwa")
        p.dma("sync", wo_b2[:], d["Wo"][1024:2048].rearrange("(k p) c -> p k c", p=128),
              writes=[wo_b2b], label=f"{tag}G4:lwb")
        rmsb_, rmsb_b = sba("rmsb", (128, 512))
        p.dma("sync", rmsb_[:], d["rms_w"].rearrange("(o c) -> o c", o=1).broadcast_to((128, 512)),
              writes=[rmsb_b], label=f"{tag}G4:ldrw")
        osb4, osb4b = sba("osb4", (128, 4, 512))
        sq4, sq4b = sba("sq4", (128, 4, 512))
        gt4, gt4b = sba("gt4", (128, 2048))
        oT4, oT4b = sba("oT4", (128, 16, 128))
        src_d, srcb = d[src], self.dbuf[src]
        dst_d, dstb_ = d[dst], self.dbuf[dst]
        for b in range(B_CORE):
            for t in range(NT):
                for h in range(4):
                    p.dma("sync", osb4[:, h, :], go_d[b, h, t * 128:(t + 1) * 128, :],
                          reads=[gob], writes=[osb4b], label=f"{tag}G4:b{b}t{t}:lo{h}")
                p.dma("sync", gt4[:], gt_d[b, t * 128:(t + 1) * 128, :],
                      reads=[gtb], writes=[gt4b], label=f"{tag}G4:b{b}t{t}:lg")
                p.op("scalar", lambda e: e.activation(out=sq4[:], in_=osb4[:], func=AF.Square),
                     reads=[osb4b], writes=[sq4b], label=f"{tag}G4:b{b}t{t}:sq")
                p.op("vector", lambda e: e.reduce_sum(
                    out=stat[:, 0:4].rearrange("p (h o) -> p h o", o=1), in_=sq4[:], axis=AX.X),
                    reads=[sq4b], writes=[statb], label=f"{tag}G4:b{b}t{t}:ss")
                p.op("vector", lambda e: e.tensor_scalar(
                    out=stat[:, 4:8], in0=stat[:, 0:4], scalar1=1.0 / 512, scalar2=1e-5,
                    op0=ALU.mult, op1=ALU.add),
                    reads=[statb], writes=[statb], label=f"{tag}G4:b{b}t{t}:mv")
                p.op("scalar", lambda e: e.activation(out=stat[:, 8:12], in_=stat[:, 4:8],
                                                      func=AF.Sqrt),
                     reads=[statb], writes=[statb], label=f"{tag}G4:b{b}t{t}:sd")
                p.op("vector", lambda e: e.reciprocal(out=stat[:, 12:16], in_=stat[:, 8:12]),
                     reads=[statb], writes=[statb], label=f"{tag}G4:b{b}t{t}:iv")
                p.op("vector", lambda e: e.tensor_tensor(
                    out=gt4[:].rearrange("p (h c) -> p h c", h=4),
                    in0=gt4[:].rearrange("p (h c) -> p h c", h=4),
                    in1=rmsb_[:].rearrange("p (o c) -> p o c", o=1).broadcast_to((128, 4, 512)),
                    op=ALU.mult),
                    reads=[gt4b, rmsb_b], writes=[gt4b], label=f"{tag}G4:b{b}t{t}:w1")
                p.op("vector", lambda e: e.tensor_tensor(
                    out=osb4[:], in0=osb4[:], in1=gt4[:].rearrange("p (h c) -> p h c", h=4),
                    op=ALU.mult),
                    reads=[osb4b, gt4b], writes=[osb4b], label=f"{tag}G4:b{b}t{t}:w2")
                p.op("vector", lambda e: e.tensor_tensor(
                    out=osb4[:], in0=osb4[:],
                    in1=stat[:, 12:16].rearrange("p (h o) -> p h o", o=1).broadcast_to((128, 4, 512)),
                    op=ALU.mult),
                    reads=[osb4b, statb], writes=[osb4b], label=f"{tag}G4:b{b}t{t}:w3")
                items = [(osb4[:].rearrange("p h c -> p (h c)")[:, i * 128:(i + 1) * 128],
                          128, 128, (oT4[:, i, :], oT4b), [osb4b]) for i in range(16)]
                self.transpose_many(items, 0, 1, f"{tag}G4:b{b}t{t}:T")
                p.dma("sync", xin0[:], src_d[b, t * 128:(t + 1) * 128, :],
                      reads=[srcb], writes=[xin0b], label=f"{tag}G4:b{b}t{t}:lx")
                for c in range(2):
                    psq, pqb = self.ps[2 + c], self.psb[2 + c]
                    makes = []
                    for k in range(16):
                        wsrc, wsrcb = (wo_a2, wo_a2b) if k < 8 else (wo_b2, wo_b2b)
                        makes.append(self.mm(psq[0:128, 0:512], oT4[:, k, :],
                                             wsrc[:, k % 8, c * 512:(c + 1) * 512],
                                             start=(k == 0), stop=(k == 15)))
                    p.group("tensor", makes, reads=[oT4b, wo_a2b, wo_b2b], writes=[pqb],
                            label=f"{tag}G4:b{b}t{t}:mm{c}")
                    p.op("vector", lambda e, c=c, psq=psq: e.tensor_add(
                        out=res[:, c * 512:(c + 1) * 512], in0=psq[0:128, 0:512],
                        in1=xin0[:, c * 512:(c + 1) * 512]),
                        reads=[pqb, xin0b], writes=[resb], label=f"{tag}G4:b{b}t{t}:rv{c}")
                p.dma("gpsimd", dst_d[b, t * 128:(t + 1) * 128, :], res[:],
                      reads=[resb], writes=[dstb_], label=f"{tag}G4:b{b}t{t}:st")

    def ffn_block(self, src, dst, final, tag):
        self.ln_transpose_phase(src, "Y2", "Y2T", f"{tag}F1")
        self.ffn_qkv(0, "Y2T", f"{tag}F2")
        self.ffn_attn(0, "Y2", "XFa", f"{tag}F3")
        self.ffn_mlp(0, "XFa", "XFb", "XFT", None, False, f"{tag}F4")
        self.ffn_qkv(1, "XFT", f"{tag}F5")
        self.ffn_attn(1, "XFb", "XFa", f"{tag}F6")
        self.ffn_mlp(1, "XFa", dst, None, src, final, f"{tag}F7")



def make_constants():
    c = {}
    c["identity"] = np.eye(128, dtype=np.float32)
    c["ones_row"] = np.ones((1, 128), dtype=np.float32)
    pj = np.arange(128)[:, None]
    fi = np.arange(128)[None, :]
    c["mask_lo"] = (fi <= pj).astype(np.float32)
    c["mask_hi"] = (fi >= pj).astype(np.float32)
    DIM_HEAD, WINDOW = 64, 512
    half = np.arange(0, DIM_HEAD, 2, dtype=np.float64)
    inv_freq = 1.0 / 10000.0 ** (half / DIM_HEAD)
    tpos = np.arange(T, dtype=np.float64)
    freqs = tpos[:, None] * inv_freq[None, :]
    freqs = np.concatenate([freqs, freqs], axis=-1)
    base = (half + 0.4 * DIM_HEAD) / (1.4 * DIM_HEAD)
    power = (tpos - T // 2) / float(WINDOW // 2)
    scale = base[None, :] ** power[:, None]
    scale = np.concatenate([scale, scale], axis=-1)
    c["rot_cos"] = np.cos(freqs).astype(np.float32)
    c["rot_sin"] = np.sin(freqs).astype(np.float32)
    c["rot_scale"] = scale.astype(np.float32)
    c["rot_iscale"] = (1.0 / scale).astype(np.float32)
    g = np.ones((1, T), dtype=np.float32)
    g[0, ::64] = 0.0
    c["gla_reset"] = g
    return c


_WMAP = {
    "in_w": "L0_attn_a_in_w", "in_b": "L0_attn_a_in_b",
    "conv_w": "L0_attn_a_conv_w", "conv_b": "L0_attn_a_conv_b",
    "xp_w": "L0_attn_a_xp_w", "xp_b": "L0_attn_a_xp_b",
    "out_w": "L0_attn_a_out_w", "out_b": "L0_attn_a_out_b",
    "Wq": "L0_attn_b_Wq", "Wk": "L0_attn_b_Wk", "Wv": "L0_attn_b_Wv",
    "Wg": "L0_attn_b_Wg", "Wgk1": "L0_attn_b_Wgk1", "Wgk2": "L0_attn_b_Wgk2",
    "bgk2": "L0_attn_b_bgk2", "rms_w": "L0_attn_b_rms_w", "Wo": "L0_attn_b_Wo",
    "wqkv": "L0_ffn_wqkv", "q_scale": "L0_ffn_q_scale", "k_scale": "L0_ffn_k_scale",
    "wo": "L0_ffn_wo", "ln_g": "L0_ffn_ln_g", "ln_b": "L0_ffn_ln_b",
    "w1": "L0_ffn_w1", "b1": "L0_ffn_b1", "w2": "L0_ffn_w2", "b2": "L0_ffn_b2",
}

_PHASES = ("p1", "ffn1", "gla", "ffn2")


def kernel(**inputs):
    n_cores = 8
    x = np.ascontiguousarray(np.asarray(inputs["L0_attn_a_x"], dtype=np.float32))
    consts = make_constants()
    mk = MK(phases=_PHASES)
    weights = {k: np.ascontiguousarray(np.asarray(inputs[v], dtype=np.float32))
               for k, v in _WMAP.items()}
    in_maps = []
    for core in range(n_cores):
        m = {"x": x[core * B_CORE:(core + 1) * B_CORE]}
        m.update(weights)
        m.update(consts)
        in_maps.append(m)
    res = run_bass_kernel_spmd(mk.nc, in_maps, core_ids=list(range(n_cores)))
    return np.concatenate([r["out"] for r in res.results], axis=0)
